# version 01
# baseline (speedup 1.0000x reference)
import math
from contextlib import ExitStack

import numpy as np
import ml_dtypes

import concourse.bass as bass
import concourse.mybir as mybir
from concourse.bass_utils import run_bass_kernel_spmd

F32 = mybir.dt.float32
BF16 = mybir.dt.bfloat16
AF = mybir.ActivationFunctionType
ALU = mybir.AluOpType
AX = mybir.AxisListType

B, S, HID = 4, 4096, 4096
NH, DH = 32, 128
INT, LAT, NE = 8192, 2048, 8
NHC, NIC, NLC = 32, 64, 16
EPS = 1e-5
SCALE = 1.0 / math.sqrt(DH)
STRIPS = [[0, 3, 4, 7], [1, 2, 5, 6]]
PC = [8, 16, 24, 32]
SLOT_BASE = [0, 8, 24, 48]
TOT_SLOTS = 80

SBUF_BASE = 16512
SBUF_TOP = 229344

PHASES_UPTO = 99
DEBUG_TAPS = ()


def _alibi_slopes():
    n = 32
    cp2 = 2 ** int(math.floor(math.log2(n)))
    base = 2.0 ** (-(2.0 ** (-(math.log2(cp2) - 3))))
    return base ** np.arange(1, cp2 + 1, dtype=np.float64)


def _diag_positions(ss):
    out = []
    for g in (STRIPS[0][ss], STRIPS[1][ss]):
        for m in range(4):
            out.append((4 * g + m, m))
    return out


def _g_table(p):
    sl = _alibi_slopes()
    cols = np.zeros((TOT_SLOTS * 128,), np.float64)
    for ss in range(4):
        gs = STRIPS[p][ss]
        for j in range(PC[ss]):
            slot = SLOT_BASE[ss] + j
            for h in range(32):
                for qb in range(4):
                    d = j - (4 * gs + qb)
                    g = 0.0 if d > 0 else math.exp(sl[h] * 64.0 * d)
                    cols[(slot * 32 + h) * 4 + qb] = g
    tbl = np.broadcast_to(cols.astype(np.float32), (128, TOT_SLOTS * 128))
    return np.ascontiguousarray(tbl)


def _mask_table(p):
    tri = np.tril(np.ones((128, 128), np.float32)).T
    ones = np.ones((128, 128), np.float32)
    blocks = []
    for ss in range(4):
        gs = STRIPS[p][ss]
        for (j, m) in _diag_positions(ss):
            blocks.append(tri if j == 4 * gs + m else ones)
    return np.ascontiguousarray(
        np.concatenate(blocks, axis=1).astype(ml_dtypes.bfloat16))


def _alibi_kk():
    sl = _alibi_slopes()
    kk = np.arange(128, dtype=np.float64) - 64.0
    return np.ascontiguousarray(np.outer(kk, sl).astype(np.float32))


class MK:
    def __init__(self, debug_taps=()):
        self.nc = bass.Bass("TRN2", target_bir_lowering=False)
        self.ctx = ExitStack()
        self.taps = set(debug_taps)
        self.ops = {e: [] for e in "tvasg"}
        self.sems = {}
        self.last_bank = {}
        self.sboff = SBUF_BASE
        self.marks = {}

    def semh(self, name):
        if name not in self.sems:
            h = self.ctx.enter_context(self.nc.semaphore(name))
            self.sems[name] = [h, 0]
        return self.sems[name]

    def cnt(self, name):
        return self.semh(name)[1]

    def wait(self, eng, name, val):
        if val <= 0:
            return
        h = self.semh(name)[0]
        self.ops[eng].append(lambda e, h=h, v=val: e.wait_ge(h, v))

    def emit(self, eng, fn, inc=None, n=1):
        if inc is None:
            self.ops[eng].append(fn)
            return None
        s = self.semh(inc)
        s[1] += n
        if eng in "va":
            self.ops[eng].append(lambda e, fn=fn: fn(e))
            self.ops[eng].append(
                lambda e, h=s[0], n=n: e.drain(fusable=True).then_inc(h, n))
        else:
            self.ops[eng].append(
                lambda e, fn=fn, h=s[0], n=n: fn(e).then_inc(h, n))
        return s[1]

    def drain_inc(self, eng, name, n=1):
        s = self.semh(name)
        s[1] += n
        self.ops[eng].append(
            lambda e, h=s[0], n=n: e.drain(fusable=True).then_inc(h, n))
        return s[1]

    def barrier(self, tag):
        name = "bar_" + tag
        for e in "tvasg":
            self.emit(e, lambda en: en.nop(), inc=name)
        v = self.cnt(name)
        for e in "tvasg":
            self.wait(e, name, v)

    def bank_wait(self, *banks):
        for b in banks:
            if b in self.last_bank:
                nm, v = self.last_bank.pop(b)
                self.wait("t", nm, v)

    def bank_used(self, b, semname):
        self.last_bank[b] = (semname, self.cnt(semname))

    def mark(self, name):
        self.marks[name] = self.sboff

    def reset(self, name):
        self.sboff = self.marks[name]

    def al(self, name, shape, dtype, align=64, at=None):
        if at is None:
            per = int(np.prod(shape[1:])) * mybir.dt.size(dtype)
            self.sboff = (self.sboff + align - 1) // align * align
            at = self.sboff
            self.sboff += per
        assert self.sboff <= SBUF_TOP, f"SBUF overflow at {name}: {self.sboff}"
        return self.nc.alloc_sbuf_tensor_at(name, list(shape), dtype, offset=at)

    def din(self, name, shape, dtype=F32):
        return self.nc.dram_tensor(name, list(shape), dtype,
                                   kind="ExternalInput").ap()

    def dscr(self, name, shape, dtype):
        kind = "ExternalOutput" if name in self.taps else "Internal"
        return self.nc.dram_tensor(name, list(shape), dtype, kind=kind).ap()

    def declare(self):
        nc = self.nc
        self.x_all = self.din("x_all", [S, HID])
        self.x_own = self.din("x_own", [2048, HID])
        self.wqkv = self.din("wqkv", [HID, 4352])
        self.wo_d = self.din("wo", [HID, HID])
        self.gatew_d = self.din("gatew", [HID, NE])
        self.upw_d = self.din("upw", [HID, INT])
        self.downw_d = self.din("downw", [INT, HID])
        self.fc1w_d = self.din("fc1w", [HID, LAT])
        self.fc2w_d = self.din("fc2w", [LAT, HID])
        self.expw_d = self.din("expw", [LAT, LAT])
        self.gamma_d = self.din("gamma_bc", [128, HID])
        self.beta_d = self.din("beta_bc", [128, HID])
        self.gtab_d = self.din("gtab", [128, TOT_SLOTS * 128])
        self.mask_d = self.din("masktab", [128, 32 * 128], BF16)

        self.y_d = nc.dram_tensor("y", [2048, HID], F32,
                                  kind="ExternalOutput").ap()

        self.ln1_scr = self.dscr("ln1_scr", [S, HID], BF16)
        self.ln1o_scr = self.dscr("ln1o_scr", [2048, HID], BF16)
        self.qt_scr = self.dscr("qt_scr", [4, 128, NH * 512], BF16)
        self.at_scr = self.dscr("at_scr", [4, 128, NHC * 512], BF16)
        self.x2_scr = self.dscr("x2_scr", [2048, HID], F32)
        self.ln2_scr = self.dscr("ln2_scr", [2048, HID], BF16)
        self.pre_scr = self.dscr("pre_scr", [2048, HID], F32)
        self.rw_scr = self.dscr("rw_scr", [1, 1024], F32)
        self.den_scr = self.dscr("den_scr", [32, 512], F32)
        self.kt2_scr = self.dscr("kt2_scr", [128, S], F32)
        self.s0_scr = self.dscr("s0_scr", [128, 512], F32)
        self.pt_scr = self.dscr("pt_scr", [8, 128, 512], F32)
        self.pe_scr = self.dscr("pe_scr", [8, 128, 512], F32)
        self.kv_scr = self.dscr("kv_scr", [2, 128, S], F32)
        self.rwdbg_scr = self.dscr("rwdbg_scr", [128, 16], F32)

        self.onesb_d = nc.inline_tensor(
            np.ones((128, 1), ml_dtypes.bfloat16), "onesb_cst")
        self.onesf_d = nc.inline_tensor(np.ones((1, 128), np.float32), "onesf_cst")
        self.identf_d = nc.inline_tensor(np.eye(128, dtype=np.float32), "identf_cst")
        self.alibi_d = nc.inline_tensor(_alibi_kk(), "alibi_cst")
        self.epsv_d = nc.inline_tensor(np.full((128, 1), EPS, np.float32), "eps_cst")

        al = self.al
        self.KT = al("KT", [128, S], BF16)
        self.VN = al("VN", [128, S], BF16)
        self.onesb = al("onesb", [128, 1], BF16)
        self.onesf = al("onesf", [1, 128], F32)
        self.identf = al("identf", [128, 128], F32)
        self.alibi = al("alibi", [128, 32], F32)
        self.epsv = al("epsv", [128, 1], F32)
        self.gatew = al("gatew_sb", [128, NHC, 8], BF16)
        self.stt = al("stt", [128, 16], F32)
        self.st2 = al("st2", [128, 64], F32)
        self.rw_nat = al("rw_nat", [128, 8], F32)
        self.rwT = al("rwT", [8, 128], F32)
        self.rw_row = al("rw_row", [1, 1024], F32)
        self.recip = al("recip", [1, 1024], F32)
        self.rb_sb = [al(f"rb_sb{i}", [128, 512], F32) for i in range(2)]
        self.flushb = al("flushb", [1, 512], F32)
        self.denst = al("denst", [1, 512], F32)
        self.dmpst = al("dmpst", [128, 512], F32)
        self.rw_bc = al("rw_bc", [128, 1024], F32)
        self.mark("shared")

        self.xa = [al(f"xa{i}", [128, HID], F32) for i in range(2)]
        self.sqst = al("sqst", [128, HID], F32)
        self.lnst = [al(f"lnst{i}", [128, HID], BF16) for i in range(2)]
        self.l1t = al("l1t", [128, NHC * 512], BF16)
        self.wkv = al("wkv", [128, NHC, 256], BF16)

        self.reset("shared")
        self.l1ot = al("l1ot", [128, NHC * 512], BF16)
        self.wq = [al(f"wq{i}", [128, NHC, 128], BF16) for i in range(2)]
        self.qT0 = al("qT0", [128, NH * 512], BF16)

        self.reset("shared")
        self.qt = al("qt", [128, NH * 512], BF16)
        self.attnT = al("attnT", [128, NHC * 512], BF16)
        self.masks = al("masks", [128, 32, 128], BF16)
        self.biasb = al("biasb", [128, 4096], F32)
        self.peh = [al(f"peh{i}", [128, 512], BF16) for i in range(2)]
        self.ptb = [al(f"ptb{i}", [128, 512], BF16) for i in range(3)]

        self.reset("shared")
        self.amac = al("amac", [128, NHC, 2, 512], BF16)
        self.wob = [al(f"wob{i}", [128, 4, 512], BF16) for i in range(2)]
        self.xsl = [al(f"xsl{i}", [128, 512], F32) for i in range(2)]
        self.x2st = [al(f"x2st{i}", [128, 512], F32) for i in range(2)]
        self.x2t = al("x2t", [128, HID], F32)
        self.sqln = al("sqln", [128, HID], F32)
        self.ln2o = al("ln2o", [128, HID], BF16)

        self.reset("shared")
        self.l2t = al("l2t", [128, NHC * 1024], BF16)
        self.hTq = al("hTq", [128, 16 * 1024], BF16)
        self.latT = al("latT", [128, NLC * 1024], BF16,
                       at=self.nc.lookup_mloc(self.hTq).addr)
        self.moeT = al("moeT", [128, NLC * 1024], BF16)
        self.wfb = [al(f"wfb{i}", [128, NHC, 128], BF16) for i in range(2)]
        self.fc2b = al("fc2b", [128, NLC, 512], BF16,
                       at=self.nc.lookup_mloc(self.wfb[0]).addr)
        self.dwb = [al(f"dwb{i}", [128, 8, 512], BF16) for i in range(2)]
        self.relust = [al(f"relust{i}", [128, 512], BF16) for i in range(2)]
        self.prest = [al(f"prest{i}", [128, 512], F32) for i in range(2)]

        self.reset("shared")
        self.pret = [al(f"pret{i}", [128, HID], F32) for i in range(2)]
        self.x2f = [al(f"x2f{i}", [128, HID], F32) for i in range(2)]
        self.gbc = al("gbc", [128, HID], F32)
        self.bbc = al("bbc", [128, HID], F32)
        self.tst = al("tst", [128, HID], F32)
        self.yst = [al(f"yst{i}", [128, HID], F32) for i in range(2)]

        self.pb = [self.ctx.enter_context(
            self.nc.psum_tensor(f"pb{i}", [128, 512], F32)) for i in range(8)]

    def chain(self, eng, name=None):
        name = name or ("ch_" + eng)
        v = self.drain_inc(eng, name)
        self.wait(eng, name, v)

    def emit_ln(self, src, sqbuf, dst, src_ready, dst_free=None):
        P = self
        nm, v = src_ready
        stt = self.stt
        P.wait("a", nm, v)
        P.emit("a", lambda e, src=src, sqbuf=sqbuf: e.activation(
            sqbuf[:], src[:], AF.Square), inc="a_sq")
        asq = P.cnt("a_sq")
        P.wait("v", nm, v)
        P.emit("v", lambda e, src=src: e.reduce_sum(
            stt[:, 0:1], src[:], axis=AX.X))
        P.wait("v", "a_sq", asq)
        P.emit("v", lambda e, sqbuf=sqbuf: e.reduce_sum(
            stt[:, 1:2], sqbuf[:], axis=AX.X))
        P.chain("v")
        P.emit("v", lambda e: e.tensor_scalar(
            stt[:, 2:3], stt[:, 0:1], 1.0 / HID, None, ALU.mult))
        P.emit("v", lambda e: e.tensor_scalar(
            stt[:, 3:4], stt[:, 1:2], 1.0 / HID, None, ALU.mult))
        P.chain("v")
        P.emit("v", lambda e: e.tensor_tensor(
            stt[:, 4:5], stt[:, 2:3], stt[:, 2:3], ALU.mult))
        P.chain("v")
        P.emit("v", lambda e: e.tensor_sub(
            stt[:, 5:6], stt[:, 3:4], stt[:, 4:5]), inc="v_var")
        P.wait("a", "v_var", P.cnt("v_var"))
        P.emit("a", lambda e: e.activation(
            stt[:, 6:7], stt[:, 5:6], AF.Ln, bias=self.epsv[:]), inc="a_ln")
        P.wait("a", "a_ln", P.cnt("a_ln"))
        P.emit("a", lambda e: e.activation(
            stt[:, 7:8], stt[:, 6:7], AF.Exp, scale=-0.5), inc="a_rstd")
        ar = P.cnt("a_rstd")
        if dst_free is not None and dst_free[1] > 0:
            P.wait("v", dst_free[0], dst_free[1])
        P.wait("v", "a_rstd", ar)
        P.emit("v", lambda e, src=src, dst=dst: e.tensor_scalar(
            dst[:], src[:], stt[:, 2:3], stt[:, 7:8],
            ALU.subtract, ALU.mult), inc="v_ln")
        return P.cnt("v_ln")

    def ph_init(self):
        P = self
        for buf, dr in ((self.onesb, self.onesb_d), (self.onesf, self.onesf_d),
                        (self.identf, self.identf_d), (self.alibi, self.alibi_d),
                        (self.epsv, self.epsv_d)):
            P.emit("s", lambda e, b=buf, d=dr: e.dma_start(b[:], d.ap()),
                   inc="dcst", n=16)
        self.CST = P.cnt("dcst")
        P.emit("g", lambda e: e.dma_start(
            self.wkv[:], self.wqkv[:, 4096:4352]
            .rearrange("(c p) j -> p c j", p=128)), inc="dwkv", n=16)
        for hc in range(NHC):
            P.emit("g", lambda e, hc=hc: e.dma_start(
                self.gatew[:, hc, :],
                self.gatew_d[hc * 128:(hc + 1) * 128, :]), inc="dgate", n=16)
        self.GATEW = P.cnt("dgate")

    def ph_A(self):
        P = self
        for e in "tvasg":
            P.wait(e, "dcst", self.CST)
            P.wait(e, "dwkv", P.cnt("dwkv"))
            P.wait(e, "dgate", self.GATEW)
        lnused = {0: 0, 1: 0}
        lnst_stored = {0: 0, 1: 0}
        kvdone = 0

        def ln_tile(src_rows_ap, dst_rows_ap, par):
            P.wait("s", "v_ln_used", lnused[par])
            P.emit("s", lambda e: e.dma_start(self.xa[par][:], src_rows_ap),
                   inc=f"dxa{par}", n=16)
            dv = P.cnt(f"dxa{par}")
            lnv = self.emit_ln(self.xa[par], self.sqst, self.lnst[par],
                               (f"dxa{par}", dv),
                               dst_free=(f"dlnst{par}", lnst_stored[par]))
            lnused[par] = P.drain_inc("v", "v_ln_used")
            P.wait("s", "v_ln", lnv)
            P.emit("s", lambda e: e.dma_start(dst_rows_ap, self.lnst[par][:]),
                   inc=f"dlnst{par}", n=16)
            lnst_stored[par] = P.cnt(f"dlnst{par}")

        for gs in range(8):
            for t4 in range(4):
                gt = gs * 4 + t4
                ln_tile(self.x_all[gt * 128:(gt + 1) * 128, :],
                        self.ln1_scr[gt * 128:(gt + 1) * 128, :], gt % 2)
                if gt == 0 and "rwdbg_scr" in self.taps:
                    P.wait("s", "v_ln", P.cnt("v_ln"))
                    P.emit("s", lambda e: e.dma_start(
                        self.rwdbg_scr[:], self.stt[:]), inc="ddbg", n=16)
                    P.wait("s", "ddbg", P.cnt("ddbg"))
                    P.wait("v", "ddbg", P.cnt("ddbg"))
                    P.wait("a", "ddbg", P.cnt("ddbg"))
            P.wait("s", f"dlnst0", lnst_stored[0])
            P.wait("s", f"dlnst1", lnst_stored[1])
            if gs > 0:
                P.wait("s", "pe_v", kvdone)
            for hc in range(NHC):
                P.emit("s", lambda e, gs=gs, hc=hc: e.dma_start(
                    self.l1t[:, hc * 512:(hc + 1) * 512],
                    self.ln1_scr[gs * 512:(gs + 1) * 512,
                                 hc * 128:(hc + 1) * 128],
                    transpose=True), inc="dl1t", n=16)
            dl1t = P.cnt("dl1t")
            P.wait("t", "dwkv", 16)
            P.wait("t", "dl1t", dl1t)
            self.bank_wait(0)
            for hc in range(NHC):
                P.emit("t", lambda e, hc=hc: e.matmul(
                    self.pb[0][:], self.wkv[:, hc, 0:128],
                    self.l1t[:, hc * 512:(hc + 1) * 512],
                    start=(hc == 0), stop=(hc == NHC - 1)),
                    inc="pe_k" if hc == NHC - 1 else None)
            P.wait("a", "pe_k", P.cnt("pe_k"))
            P.emit("a", lambda e, gs=gs: e.activation(
                self.KT[:, gs * 512:(gs + 1) * 512], self.pb[0][:],
                AF.Copy, scale=SCALE), inc="a_kev")
            self.bank_used(0, "a_kev")
            for t4 in range(4):
                self.bank_wait(1)
                for hc in range(NHC):
                    P.emit("t", lambda e, hc=hc, t4=t4: e.matmul(
                        self.pb[1][:, 0:128],
                        self.l1t[:, hc * 512 + t4 * 128:
                                 hc * 512 + (t4 + 1) * 128],
                        self.wkv[:, hc, 128:256],
                        start=(hc == 0), stop=(hc == NHC - 1)),
                        inc="pe_v" if hc == NHC - 1 else None)
                P.wait("v", "pe_v", P.cnt("pe_v"))
                P.emit("v", lambda e, gs=gs, t4=t4: e.tensor_copy(
                    self.VN[:, (gs * 4 + t4) * 128:(gs * 4 + t4 + 1) * 128],
                    self.pb[1][:, 0:128]), inc="v_vev")
                self.bank_used(1, "v_vev")
            kvdone = P.cnt("pe_v")
            KVSEM = "pe_v"

        for ot in range(16):
            ln_tile(self.x_own[ot * 128:(ot + 1) * 128, :],
                    self.ln1o_scr[ot * 128:(ot + 1) * 128, :], ot % 2)
        self.DLNO = P.cnt("dlnst0") + P.cnt("dlnst1")

        self.LNST0 = P.cnt("dlnst0")
        self.LNST1 = P.cnt("dlnst1")

        if "kv_scr" in self.taps:
            P.wait("v", "a_kev", P.cnt("a_kev"))
            P.emit("v", lambda e: e.tensor_copy(self.sqst[:, 0:4096],
                                                self.KT[:]), inc="v_dbg")
            P.wait("s", "v_dbg", P.cnt("v_dbg"))
            P.emit("s", lambda e: e.dma_start(self.kv_scr[0], self.sqst[:, 0:4096]),
                   inc="ddbg", n=16)
            P.wait("v", "ddbg", P.cnt("ddbg"))
            P.emit("v", lambda e: e.tensor_copy(self.sqst[:, 0:4096],
                                                self.KT[:]), inc="v_dbg")
            P.wait("s", "v_dbg", P.cnt("v_dbg"))
            P.emit("s", lambda e: e.dma_start(self.kv_scr[1], self.sqst[:, 0:4096]),
                   inc="ddbg", n=16)
            P.wait("s", "ddbg", P.cnt("ddbg"))

    def ph_B0(self):
        P = self
        qbdone = 0
        qmm = {0: 0, 1: 0}
        for ss in range(4):
            if ss > 0:
                P.wait("s", "pe_qmm", qbdone)
                P.wait("a", "dqts", P.cnt("dqts"))
            P.wait("s", "dlnst0", self.LNST0)
            P.wait("s", "dlnst1", self.LNST1)
            for hc in range(NHC):
                P.emit("s", lambda e, ss=ss, hc=hc: e.dma_start(
                    self.l1ot[:, hc * 512:(hc + 1) * 512],
                    self.ln1o_scr[ss * 512:(ss + 1) * 512,
                                  hc * 128:(hc + 1) * 128],
                    transpose=True), inc="dl1ot", n=16)
            dl1ot = P.cnt("dl1ot")
            for oc in range(NH):
                par = oc % 2
                if qmm[par] == 0:
                    P.wait("g", "dlnst0", self.LNST0)
                    P.wait("g", "dlnst1", self.LNST1)
                P.wait("g", "pe_qmm", qmm[par])
                P.emit("g", lambda e, oc=oc, par=par: e.dma_start(
                    self.wq[par][:],
                    self.wqkv[:, oc * 128:(oc + 1) * 128]
                    .rearrange("(c p) j -> p c j", p=128)),
                    inc=f"dwq{par}", n=16)
                P.wait("t", f"dwq{par}", P.cnt(f"dwq{par}"))
                P.wait("t", "dl1ot", dl1ot)
                self.bank_wait(par)
                for hc in range(NHC):
                    P.emit("t", lambda e, hc=hc, par=par: e.matmul(
                        self.pb[par][:], self.wq[par][:, hc, :],
                        self.l1ot[:, hc * 512:(hc + 1) * 512],
                        start=(hc == 0), stop=(hc == NHC - 1)),
                        inc="pe_qmm" if hc == NHC - 1 else None)
                qmm[par] = P.cnt("pe_qmm")
                P.wait("a", "pe_qmm", qmm[par])
                P.emit("a", lambda e, oc=oc, par=par: e.activation(
                    self.qT0[:, oc * 512:(oc + 1) * 512], self.pb[par][:],
                    AF.Copy), inc="a_qev")
                self.bank_used(par, "a_qev")
            P.wait("a", "a_qev", P.cnt("a_qev"))
            P.drain_inc("a", "a_qfl")
            P.wait("s", "a_qfl", P.cnt("a_qfl"))
            P.emit("s", lambda e, ss=ss: e.dma_start(
                self.qt_scr[ss], self.qT0[:]), inc="dqts", n=16)
            qbdone = P.cnt("pe_qmm")
        self.DQTS = P.cnt("dqts")

    def ph_B1(self):
        P = self
        P.wait("t", "dcst", self.CST)
        P.wait("s", "dqts", self.DQTS)
        P.emit("s", lambda e: e.dma_start(self.masks[:].rearrange(
            "p a b -> p (a b)"), self.mask_d), inc="dmask", n=16)
        P.wait("v", "dmask", P.cnt("dmask"))
        if "kt2_scr" in self.taps:
            for ch in range(8):
                P.wait("v", "dkt2", P.cnt("dkt2"))
                P.emit("v", lambda e, ch=ch: e.tensor_copy(
                    self.dmpst[:], self.KT[:, ch * 512:(ch + 1) * 512]))
                P.emit("v", lambda e: e.tensor_copy(
                    self.stt[:, 13:14], self.dmpst[:, 511:512]), inc="v_kt2")
                P.wait("s", "v_kt2", P.cnt("v_kt2"))
                P.emit("s", lambda e, ch=ch: e.dma_start(
                    self.kt2_scr[:, ch * 512:(ch + 1) * 512], self.dmpst[:]),
                    inc="dkt2", n=16)
        exp_at_sbank = {2: 0, 3: 0}
        pt_free = {0: 0, 1: 0, 2: 0}
        peh_free = {0: 0, 1: 0}
        avn_free = {0: 0, 1: 0}
        den_free = {0: 0, 1: 0}
        rb_free = {0: 0, 1: 0}
        rbmm_done = {0: 0, 1: 0}
        smm_all = 0
        dens = {}

        def emit_tail(h):
            hp = h % 2
            ss0 = self._cur_ss == 0
            avb = 4 if hp == 0 else 6
            deb = 5 if hp == 0 else 7
            rbb = 0 if hp == 0 else 1
            if ss0 and "den_scr" in self.taps:
                P.wait("a", "pe_ptc", dens[h])
                deb_ = 5 if hp == 0 else 7
                P.emit("a", lambda e, deb_=deb_: e.activation(
                    self.denst[:], self.pb[deb_][0:1, :], AF.Copy))
                P.emit("a", lambda e: e.activation(
                    self.stt[0:1, 12:13], self.denst[0:1, 511:512], AF.Copy),
                    inc="a_dden")
                P.wait("s", "a_dden", P.cnt("a_dden"))
                P.emit("s", lambda e, h=h: e.dma_start(
                    self.den_scr[h:h + 1, :], self.denst[:]), inc="ddump2",
                    n=16)
                P.wait("v", "ddump2", P.cnt("ddump2"))
            P.wait("v", "pe_ptc", dens[h])
            if rb_free[hp]:
                P.wait("v", "pe_rb", rbmm_done[hp])
            P.emit("v", lambda e, deb=deb, hp=hp: e.reciprocal(
                self.recip[0:1, hp * 512:(hp + 1) * 512],
                self.pb[deb][0:1, :]), inc="v_recip")
            den_free[hp] = P.cnt("v_recip")
            self.bank_used(deb, "v_recip")
            P.wait("t", "v_recip", P.cnt("v_recip"))
            self.bank_wait(rbb)
            P.emit("t", lambda e, rbb=rbb, hp=hp: e.matmul(
                self.pb[rbb][:], self.onesf[:],
                self.recip[0:1, hp * 512:(hp + 1) * 512],
                start=True, stop=True), inc="pe_rb")
            rbmm_done[hp] = P.cnt("pe_rb")
            P.wait("a", "pe_rb", P.cnt("pe_rb"))
            P.emit("a", lambda e, rbb=rbb, hp=hp: e.activation(
                self.rb_sb[hp][:], self.pb[rbb][:], AF.Copy), inc="a_rbev")
            rb_free[hp] = P.cnt("a_rbev")
            self.bank_used(rbb, "a_rbev")
            P.wait("v", "a_rbev", rb_free[hp])
            P.emit("v", lambda e, avb=avb, h=h, hp=hp: e.tensor_tensor(
                self.attnT[:, h * 512:(h + 1) * 512], self.pb[avb][:],
                self.rb_sb[hp][:], ALU.mult), inc="v_norm")
            avn_free[hp] = P.cnt("v_norm")
            self.bank_used(avb, "v_norm")

        for ss in range(4):
            self._cur_ss = ss
            npc = PC[ss]
            dpos = _diag_positions(ss)
            if ss > 0:
                P.wait("s", "pe_ptc", smm_all)
                P.wait("s", "dats", P.cnt("dats"))
            P.wait("s", "dqts", self.DQTS)
            P.emit("s", lambda e, ss=ss: e.dma_start(
                self.qt[:], self.qt_scr[ss]), inc="dqt", n=16)
            dqt = P.cnt("dqt")
            c0 = SLOT_BASE[ss] * 128
            c1 = (SLOT_BASE[ss] + npc) * 128
            P.emit("s", lambda e, c0=c0, c1=c1: e.dma_start(
                self.biasb[:, 0:c1 - c0], self.gtab_d[:, c0:c1]),
                inc="dbias", n=16)
            dbias = P.cnt("dbias")
            P.wait("v", "dats", P.cnt("dats"))
            P.wait("v", "dbias", dbias)
            for h in range(NH):
                hp = h % 2
                avb = 4 if hp == 0 else 6
                deb = 5 if hp == 0 else 7
                if avn_free[hp]:
                    P.wait("t", "v_norm", avn_free[hp])
                if den_free[hp]:
                    P.wait("t", "v_recip", den_free[hp])
                self.last_bank.pop(avb, None)
                self.last_bank.pop(deb, None)
                for j in range(npc):
                    sbk = 2 + (j % 2)
                    pti = j % 3
                    if j < 2 and h == 0:
                        P.wait("t", "dqt", dqt)
                    if exp_at_sbank[sbk]:
                        P.wait("t", "a_exp", exp_at_sbank[sbk])
                    P.emit("t", lambda e, j=j, h=h, sbk=sbk: e.matmul(
                        self.pb[sbk][:], self.KT[:, j * 128:(j + 1) * 128],
                        self.qt[:, h * 512:(h + 1) * 512],
                        start=True, stop=True), inc="pe_s")
                    P.wait("a", "pe_s", P.cnt("pe_s"))
                    if ss == 0 and j == 0 and h == 0:
                        P.wait("a", "dcst", self.CST)
                        if "s0_scr" in self.taps:
                            P.emit("a", lambda e, sbk=sbk: e.activation(
                                self.denst[:]
                                if False else self.dmpst[:],
                                self.pb[sbk][:], AF.Copy))
                            P.emit("a", lambda e: e.activation(
                                self.stt[:, 12:13], self.dmpst[:, 511:512],
                                AF.Copy), inc="a_s0")
                            P.wait("s", "a_s0", P.cnt("a_s0"))
                            P.emit("s", lambda e: e.dma_start(
                                self.s0_scr, self.dmpst[:]), inc="ds0", n=16)
                            P.wait("v", "ds0", P.cnt("ds0"))
                    pp = j % 2
                    if peh_free[pp]:
                        P.wait("a", "v_p", peh_free[pp])
                    P.emit("a", lambda e, sbk=sbk, h=h, pp=pp: e.activation(
                        self.peh[pp][:], self.pb[sbk][:], AF.Exp,
                        bias=self.alibi[:, h:h + 1]), inc="a_exp")
                    exp_at_sbank[sbk] = P.cnt("a_exp")
                    P.wait("v", "a_exp", exp_at_sbank[sbk])
                    if ss == 0 and j == 0 and h == 0:
                        P.wait("v", "dcst", self.CST)
                    if pt_free[pti]:
                        P.wait("v", "pe_ptc", pt_free[pti])
                    for qb in range(4):
                        gc = (j * 32 + h) * 4 + qb
                        P.emit("v", lambda e, pti=pti, pp=pp, qb=qb, gc=gc:
                               e.tensor_scalar(
                                   self.ptb[pti][:, qb * 128:(qb + 1) * 128],
                                   self.peh[pp][:, qb * 128:(qb + 1) * 128],
                                   self.biasb[:, gc:gc + 1],
                                   self.biasb[:, gc:gc + 1],
                                   ALU.mult, ALU.mult))
                    if any(dj == j for dj, _ in dpos):
                        P.chain("v")
                    for i, (dj, dm) in enumerate(dpos):
                        if dj == j:
                            P.emit("v", lambda e, pti=pti, dm=dm, ss=ss, i=i:
                                   e.tensor_tensor(
                                       self.ptb[pti][:, dm * 128:(dm + 1) * 128],
                                       self.ptb[pti][:, dm * 128:(dm + 1) * 128],
                                       self.masks[:, ss * 8 + i, :], ALU.mult))
                    peh_free[pp] = P.drain_inc("v", "v_p")
                    if ss == 0 and h == 0 and j < 8 and "pt_scr" in self.taps:
                        P.wait("v", "ddump", P.cnt("ddump"))
                        P.emit("v", lambda e, pti=pti: e.tensor_copy(
                            self.dmpst[:], self.ptb[pti][:]))
                        P.emit("v", lambda e: e.tensor_copy(
                            self.stt[:, 13:14], self.dmpst[:, 511:512]),
                            inc="v_dmp")
                        P.wait("s", "v_dmp", P.cnt("v_dmp"))
                        P.emit("s", lambda e, j=j: e.dma_start(
                            self.pt_scr[j], self.dmpst[:]), inc="ddump", n=16)
                        P.wait("v", "ddump", P.cnt("ddump"))
                        P.emit("v", lambda e, pp=pp: e.tensor_copy(
                            self.dmpst[:], self.peh[pp][:]))
                        P.emit("v", lambda e: e.tensor_copy(
                            self.stt[:, 13:14], self.dmpst[:, 511:512]),
                            inc="v_dmp")
                        P.wait("s", "v_dmp", P.cnt("v_dmp"))
                        P.emit("s", lambda e, j=j: e.dma_start(
                            self.pe_scr[j], self.dmpst[:]), inc="ddump", n=16)
                    P.wait("t", "v_p", P.cnt("v_p"))
                    P.emit("t", lambda e, deb=deb, pti=pti, j=j, npc=npc:
                           e.matmul(self.pb[deb][0:1, :], self.onesb[:],
                                    self.ptb[pti][:],
                                    start=(j == 0), stop=(j == npc - 1)))
                    P.emit("t", lambda e, avb=avb, pti=pti, j=j, npc=npc:
                           e.matmul(self.pb[avb][:],
                                    self.VN[:, j * 128:(j + 1) * 128],
                                    self.ptb[pti][:],
                                    start=(j == 0), stop=(j == npc - 1)),
                           inc="pe_ptc")
                    pt_free[pti] = P.cnt("pe_ptc")
                dens[h] = P.cnt("pe_ptc")
                if h > 0:
                    emit_tail(h - 1)
            emit_tail(NH - 1)
            smm_all = P.cnt("pe_ptc")
            P.wait("v", "v_norm", P.cnt("v_norm"))
            P.drain_inc("v", "v_atfl")
            P.wait("s", "v_atfl", P.cnt("v_atfl"))
            P.emit("s", lambda e, ss=ss: e.dma_start(
                self.at_scr[ss], self.attnT[:]), inc="dats", n=16)
        self.DATS = P.cnt("dats")

    def ph_C(self, mb):
        P = self
        P.wait("s", "dats", self.DATS)
        for k in range(2):
            ss = 2 * mb + k
            P.emit("s", lambda e, ss=ss, k=k: e.dma_start(
                self.amac[:, :, k, :],
                self.at_scr[ss].rearrange("p (c j) -> p c j", j=512)),
                inc="damac", n=16)
        damac = P.cnt("damac")
        P.wait("t", "damac", damac)
        if not hasattr(self, "_wo_gchunk"):
            self._wo_gchunk = 0
        xsl_used = {0: 0, 1: 0}
        x2st_stored = {0: 0, 1: 0}
        for ocs in range(8):
            dwo_val = []
            for c4 in range(8):
                par = c4 % 2
                g = self._wo_gchunk
                self._wo_gchunk += 1
                P.wait("g", "pe_woc", g - 1)
                P.emit("g", lambda e, c4=c4, ocs=ocs, par=par: e.dma_start(
                    self.wob[par][:],
                    self.wo_d[c4 * 512:(c4 + 1) * 512,
                              ocs * 512:(ocs + 1) * 512]
                    .rearrange("(c p) j -> p c j", p=128)),
                    inc=f"dwo{par}", n=16)
                dwo_val.append(P.cnt(f"dwo{par}"))
            self.bank_wait(*range(8))
            for hc in range(NHC):
                c4 = hc // 4
                par = c4 % 2
                if hc % 4 == 0:
                    P.wait("t", f"dwo{par}", dwo_val[c4])
                for tb in range(8):
                    inc = None
                    if tb == 7 and hc % 4 == 3:
                        inc = "pe_woc"
                    P.emit("t", lambda e, hc=hc, tb=tb, par=par: e.matmul(
                        self.pb[tb][:], self.amac[:, hc, tb // 4,
                                                  (tb % 4) * 128:
                                                  (tb % 4 + 1) * 128],
                        self.wob[par][:, hc % 4, :],
                        start=(hc == 0), stop=(hc == NHC - 1)),
                        inc=inc)
            pewo_woc = P.cnt("pe_woc")
            for tb in range(8):
                par = tb % 2
                P.wait("s", "v_x2", xsl_used[par])
                P.emit("s", lambda e, tb=tb, ocs=ocs, mb=mb, par=par:
                       e.dma_start(self.xsl[par][:],
                                   self.x_own[(mb * 8 + tb) * 128:
                                              (mb * 8 + tb + 1) * 128,
                                              ocs * 512:(ocs + 1) * 512]),
                       inc=f"dxsl{par}", n=16)
                P.wait("v", f"dxsl{par}", P.cnt(f"dxsl{par}"))
                P.wait("v", "pe_woc", pewo_woc)
                P.wait("v", f"dx2s{par}", x2st_stored[par])
                P.emit("v", lambda e, tb=tb, par=par: e.tensor_add(
                    self.x2st[par][:], self.pb[tb][:], self.xsl[par][:]),
                    inc="v_x2")
                xsl_used[par] = P.cnt("v_x2")
                self.bank_used(tb, "v_x2")
                P.wait("s", "v_x2", P.cnt("v_x2"))
                P.emit("s", lambda e, tb=tb, ocs=ocs, mb=mb, par=par:
                       e.dma_start(self.x2_scr[(mb * 8 + tb) * 128:
                                               (mb * 8 + tb + 1) * 128,
                                               ocs * 512:(ocs + 1) * 512],
                                   self.x2st[par][:]), inc=f"dx2s{par}", n=16)
                x2st_stored[par] = P.cnt(f"dx2s{par}")
        P.wait("v", "dy0", P.cnt("dy0"))
        P.wait("v", "dy1", P.cnt("dy1"))
        for tb in range(8):
            P.wait("s", "dx2s0", P.cnt("dx2s0"))
            P.wait("s", "dx2s1", P.cnt("dx2s1"))
            if tb > 0:
                P.wait("s", "v_ln_used", P.cnt("v_ln_used"))
            P.emit("s", lambda e, tb=tb, mb=mb: e.dma_start(
                self.x2t[:], self.x2_scr[(mb * 8 + tb) * 128:
                                         (mb * 8 + tb + 1) * 128, :]),
                inc="dx2t", n=16)
            lnv = self.emit_ln(self.x2t, self.sqln, self.ln2o,
                               ("dx2t", P.cnt("dx2t")),
                               dst_free=("dln2", P.cnt("dln2")))
            P.drain_inc("v", "v_ln_used")
            P.wait("s", "v_ln", lnv)
            P.emit("s", lambda e, tb=tb, mb=mb: e.dma_start(
                self.ln2_scr[(mb * 8 + tb) * 128:(mb * 8 + tb + 1) * 128, :],
                self.ln2o[:]), inc="dln2", n=16)
        self.DLN2 = P.cnt("dln2")

    def ph_D(self, mb):
        P = self
        P.wait("s", "dln2", self.DLN2)
        for hc in range(NHC):
            P.emit("s", lambda e, hc=hc, mb=mb: e.dma_start(
                self.l2t[:, hc * 1024:(hc + 1) * 1024],
                self.ln2_scr[mb * 1024:(mb + 1) * 1024,
                             hc * 128:(hc + 1) * 128],
                transpose=True), inc="dl2t", n=16)
        dl2t = P.cnt("dl2t")

        P.wait("t", "dl2t", dl2t)
        P.wait("t", "dgate", self.GATEW)
        for tb in range(8):
            self.bank_wait(7)
            for hc in range(NHC):
                P.emit("t", lambda e, hc=hc, tb=tb: e.matmul(
                    self.pb[7][:, 0:8],
                    self.l2t[:, hc * 1024 + tb * 128:hc * 1024 + (tb + 1) * 128],
                    self.gatew[:, hc, :],
                    start=(hc == 0), stop=(hc == NHC - 1)),
                    inc="pe_g" if hc == NHC - 1 else None)
            P.wait("v", "pe_g", P.cnt("pe_g"))
            self.emit_top2(tb)
            self.bank_used(7, "v_rw")
        P.wait("t", "v_rw", P.cnt("v_rw"))
        self.bank_wait(6)
        P.emit("t", lambda e: e.transpose(
            self.pb[6][0:8, 0:128], self.rw_nat[:], self.identf[:]),
            inc="pe_rwt")
        P.wait("a", "pe_rwt", P.cnt("pe_rwt"))
        P.emit("a", lambda e: e.activation(
            self.rwT[:], self.pb[6][0:8, 0:128], AF.Copy), inc="a_rwt")
        self.bank_used(6, "a_rwt")
        P.wait("s", "a_rwt", P.cnt("a_rwt"))
        P.emit("s", lambda e: e.dma_start(
            self.rw_scr.rearrange("o (a b) -> (o a) b", b=128), self.rwT[:]),
            inc="drws", n=16)
        P.wait("s", "drws", P.cnt("drws"))
        P.emit("s", lambda e: e.dma_start(self.rw_row[:], self.rw_scr),
               inc="drwl", n=16)
        P.wait("t", "drwl", P.cnt("drwl"))
        for half in range(2):
            self.bank_wait(6)
            P.emit("t", lambda e, half=half: e.matmul(
                self.pb[6][:], self.onesf[:],
                self.rw_row[0:1, half * 512:(half + 1) * 512],
                start=True, stop=True), inc="pe_rwb")
            P.wait("a", "pe_rwb", P.cnt("pe_rwb"))
            P.emit("a", lambda e, half=half: e.activation(
                self.rw_bc[:, half * 512:(half + 1) * 512],
                self.pb[6][:], AF.Copy), inc="a_rwb")
            self.bank_used(6, "a_rwb")
        ARWB = P.cnt("a_rwb")
        if "rwdbg_scr" in self.taps and mb == 0:
            P.wait("s", "a_rwb", ARWB)
            P.emit("s", lambda e: e.dma_start(
                self.rwdbg_scr[:, 0:8], self.rw_nat[:]), inc="ddbg", n=16)

        f1_used = {0: 0, 1: 0}
        for lc in range(NLC):
            par = lc % 2
            P.wait("g", "pe_f1", f1_used[par])
            P.emit("g", lambda e, lc=lc, par=par: e.dma_start(
                self.wfb[par][:],
                self.fc1w_d[:, lc * 128:(lc + 1) * 128]
                .rearrange("(c p) j -> p c j", p=128)), inc=f"dwf{par}", n=16)
            P.wait("t", f"dwf{par}", P.cnt(f"dwf{par}"))
            self.bank_wait(2 * par, 2 * par + 1)
            for hc in range(NHC):
                for half in range(2):
                    P.emit("t", lambda e, hc=hc, half=half, par=par: e.matmul(
                        self.pb[2 * par + half][:], self.wfb[par][:, hc, :],
                        self.l2t[:, hc * 1024 + half * 512:
                                 hc * 1024 + (half + 1) * 512],
                        start=(hc == 0), stop=(hc == NHC - 1)),
                        inc="pe_f1" if (hc == NHC - 1 and half == 1) else None)
            f1_used[par] = P.cnt("pe_f1")
            P.wait("a", "pe_f1", P.cnt("pe_f1"))
            for half in range(2):
                P.emit("a", lambda e, lc=lc, half=half, par=par: e.activation(
                    self.latT[:, lc * 1024 + half * 512:
                              lc * 1024 + (half + 1) * 512],
                    self.pb[2 * par + half][:], AF.Copy), inc="a_f1")
            self.bank_used(2 * par, "a_f1")
            self.bank_used(2 * par + 1, "a_f1")
        PEF1 = P.cnt("pe_f1")
        AF1FL = P.cnt("a_f1")

        ex_used = {0: 0, 1: 0}
        for mc in range(NLC):
            par = mc % 2
            if mc < 2:
                P.wait("g", "pe_f1", PEF1)
            else:
                P.wait("g", "pe_ex", ex_used[par])
            P.emit("g", lambda e, mc=mc, par=par: e.dma_start(
                self.wfb[par][:, 0:NLC, :],
                self.expw_d[:, mc * 128:(mc + 1) * 128]
                .rearrange("(c p) j -> p c j", p=128)), inc=f"dwe{par}", n=16)
            P.wait("t", f"dwe{par}", P.cnt(f"dwe{par}"))
            if mc == 0:
                P.wait("t", "a_f1", AF1FL)
            self.bank_wait(4, 5)
            for lc in range(NLC):
                for half in range(2):
                    P.emit("t", lambda e, lc=lc, half=half, par=par: e.matmul(
                        self.pb[4 + half][:], self.wfb[par][:, lc, :],
                        self.latT[:, lc * 1024 + half * 512:
                                  lc * 1024 + (half + 1) * 512],
                        start=(lc == 0), stop=(lc == NLC - 1)),
                        inc="pe_ex" if (lc == NLC - 1 and half == 1) else None)
            ex_used[par] = P.cnt("pe_ex")
            P.wait("v", "pe_ex", P.cnt("pe_ex"))
            if mc == 0:
                P.wait("v", "a_rwb", ARWB)
            for half in range(2):
                P.emit("v", lambda e, mc=mc, half=half: e.tensor_tensor(
                    self.moeT[:, mc * 1024 + half * 512:
                              mc * 1024 + (half + 1) * 512],
                    self.pb[4 + half][:],
                    self.rw_bc[:, half * 512:(half + 1) * 512], ALU.mult),
                    inc="v_moe")
            self.bank_used(4, "v_moe")
            self.bank_used(5, "v_moe")
        PEEX = P.cnt("pe_ex")
        VMOE = P.drain_inc("v", "v_moef")

        pre_base = {0: P.cnt("dpre0"), 1: P.cnt("dpre1")}
        up_used = {0: 0, 1: 0}
        re_used = {0: 0, 1: 0}
        pre_used = {0: 0, 1: 0}
        for q in range(4):
            dn_before = P.cnt("pe_dn")
            for il in range(16):
                par = il % 2
                if q == 0 and il < 2:
                    P.wait("g", "pe_ex", PEEX)
                else:
                    P.wait("g", "pe_up", up_used[par])
                P.emit("g", lambda e, q=q, il=il, par=par: e.dma_start(
                    self.wfb[par][:],
                    self.upw_d[:, (16 * q + il) * 128:(16 * q + il + 1) * 128]
                    .rearrange("(c p) j -> p c j", p=128)),
                    inc=f"dwu{par}", n=16)
                P.wait("t", f"dwu{par}", P.cnt(f"dwu{par}"))
                self.bank_wait(4, 5)
                for hc in range(NHC):
                    for half in range(2):
                        P.emit("t", lambda e, hc=hc, half=half, par=par:
                               e.matmul(self.pb[4 + half][:],
                                        self.wfb[par][:, hc, :],
                                        self.l2t[:, hc * 1024 + half * 512:
                                                 hc * 1024 + (half + 1) * 512],
                                        start=(hc == 0), stop=(hc == NHC - 1)),
                               inc="pe_up" if (hc == NHC - 1 and half == 1)
                               else None)
                up_used[par] = P.cnt("pe_up")
                P.wait("a", "pe_up", P.cnt("pe_up"))
                for half in range(2):
                    rpar = half
                    P.wait("a", "v_hsq", re_used[rpar])
                    P.emit("a", lambda e, half=half, rpar=rpar: e.activation(
                        self.relust[rpar][:], self.pb[4 + half][:], AF.Relu),
                        inc="a_re")
                    P.wait("v", "a_re", P.cnt("a_re"))
                    if il == 0 and half == 0:
                        if q == 0:
                            P.wait("v", "pe_ex", PEEX)
                        else:
                            P.wait("v", "pe_dn", dn_before)
                    P.emit("v", lambda e, il=il, half=half, rpar=rpar:
                           e.tensor_mul(
                               self.hTq[:, il * 1024 + half * 512:
                                        il * 1024 + (half + 1) * 512],
                               self.relust[rpar][:], self.relust[rpar][:]),
                           inc="v_hsq")
                    re_used[rpar] = P.cnt("v_hsq")
                self.bank_used(4, "a_re")
                self.bank_used(5, "a_re")
            VHSQ = P.drain_inc("v", "v_hsqf")

            dw_used = {0: 0, 1: 0}
            for ocs in range(8):
                dwv = {}
                for c8 in range(2):
                    P.wait("g", "pe_dn", dw_used[c8])
                    P.emit("g", lambda e, q=q, ocs=ocs, c8=c8: e.dma_start(
                        self.dwb[c8][:],
                        self.downw_d[(16 * q + 8 * c8) * 128:
                                     (16 * q + 8 * c8 + 8) * 128,
                                     ocs * 512:(ocs + 1) * 512]
                        .rearrange("(c p) j -> p c j", p=128)),
                        inc=f"ddw{c8}", n=16)
                    dwv[c8] = P.cnt(f"ddw{c8}")
                if q == 3:
                    if ocs == 0:
                        P.wait("g", "pe_up", P.cnt("pe_up"))
                    else:
                        P.wait("g", "pe_dn", dw_used[0])
                    P.emit("g", lambda e, ocs=ocs: e.dma_start(
                        self.fc2b[:],
                        self.fc2w_d[:, ocs * 512:(ocs + 1) * 512]
                        .rearrange("(c p) j -> p c j", p=128)),
                        inc="dfc2", n=16)
                if ocs == 0:
                    P.wait("t", "v_hsqf", VHSQ)
                self.bank_wait(*range(8))
                for il in range(16):
                    c8 = il // 8
                    if il % 8 == 0:
                        P.wait("t", f"ddw{c8}", dwv[c8])
                    for tb in range(8):
                        inc = ("pe_dn" if (q != 3 and il == 15 and tb == 7)
                               else None)
                        P.emit("t", lambda e, il=il, tb=tb, c8=c8, q=q:
                               e.matmul(
                                   self.pb[tb][:],
                                   self.hTq[:, il * 1024 + tb * 128:
                                            il * 1024 + (tb + 1) * 128],
                                   self.dwb[c8][:, il % 8, :],
                                   start=(il == 0),
                                   stop=(q != 3 and il == 15)), inc=inc)
                if q == 3:
                    P.wait("t", "dfc2", P.cnt("dfc2"))
                    if ocs == 0:
                        P.wait("t", "v_moef", VMOE)
                    for lc in range(NLC):
                        for tb in range(8):
                            P.emit("t", lambda e, lc=lc, tb=tb: e.matmul(
                                self.pb[tb][:],
                                self.moeT[:, lc * 1024 + tb * 128:
                                          lc * 1024 + (tb + 1) * 128],
                                self.fc2b[:, lc, :],
                                start=False, stop=(lc == NLC - 1)),
                                inc="pe_dn" if (lc == NLC - 1 and tb == 7)
                                else None)
                pednv = P.cnt("pe_dn")
                dw_used[0] = pednv
                dw_used[1] = pednv
                for tb in range(8):
                    par = tb % 2
                    P.wait("a", "pe_dn", pednv)
                    P.wait("a", f"dpre{par}", pre_used[par])
                    P.emit("a", lambda e, tb=tb, par=par: e.activation(
                        self.prest[par][:], self.pb[tb][:], AF.Copy),
                        inc="a_pre")
                    self.bank_used(tb, "a_pre")
                    P.wait("g", "a_pre", P.cnt("a_pre"))
                    if q > 0 and ocs == 0 and tb == 0:
                        P.wait("g", "dpre0", pre_base[0] + 16 * 32 * q)
                        P.wait("g", "dpre1", pre_base[1] + 16 * 32 * q)
                    op = {} if q == 0 else {"accum_op": ALU.add}
                    P.emit("g", lambda e, tb=tb, ocs=ocs, mb=mb, par=par,
                           op=op: e.dma_start(
                               self.pre_scr[(mb * 8 + tb) * 128:
                                            (mb * 8 + tb + 1) * 128,
                                            ocs * 512:(ocs + 1) * 512],
                               self.prest[par][:], **op), inc=f"dpre{par}",
                           n=16)
                    pre_used[par] = P.cnt(f"dpre{par}")
        self.DPRE = (P.cnt("dpre0"), P.cnt("dpre1"))

    def emit_top2(self, tb):
        P = self
        l = self.pb[7]
        s2 = self.st2
        P.emit("v", lambda e: e.tensor_copy(s2[:, 0:1], l[:, 0:1]))
        for i in range(1, 7):
            P.chain("v")
            P.emit("v", lambda e, i=i: e.tensor_max(
                s2[:, i:i + 1], s2[:, i - 1:i], l[:, i:i + 1]))
        P.emit("v", lambda e: e.tensor_copy(s2[:, 13:14], l[:, 7:8]))
        for j in range(6, 0, -1):
            P.chain("v")
            P.emit("v", lambda e, j=j: e.tensor_max(
                s2[:, 6 + j:7 + j], s2[:, 7 + j:8 + j], l[:, j:j + 1]))
        P.chain("v")
        P.emit("v", lambda e: e.tensor_add(s2[:, 16:17], l[:, 0:1], s2[:, 7:8]))
        P.emit("v", lambda e: e.tensor_add(s2[:, 23:24], l[:, 7:8], s2[:, 6:7]))
        for i in range(1, 7):
            P.emit("v", lambda e, i=i: e.tensor_max(
                s2[:, 24 + i:25 + i], s2[:, i - 1:i], s2[:, 7 + i:8 + i]))
        P.chain("v")
        for i in range(1, 7):
            P.emit("v", lambda e, i=i: e.tensor_add(
                s2[:, 16 + i:17 + i], l[:, i:i + 1], s2[:, 24 + i:25 + i]))
        P.chain("v")
        P.emit("v", lambda e: e.tensor_max(s2[:, 32:33], s2[:, 16:17], s2[:, 17:18]))
        for i in range(2, 8):
            P.chain("v")
            P.emit("v", lambda e, i=i: e.tensor_max(
                s2[:, 32:33], s2[:, 32:33], s2[:, 16 + i:17 + i]))
        P.chain("v")
        P.emit("v", lambda e, tb=tb: e.tensor_copy(
            self.rw_nat[:, tb:tb + 1], s2[:, 32:33]), inc="v_rw")

    def ph_D5(self, mb):
        P = self
        P.emit("s", lambda e: e.dma_start(self.gbc[:], self.gamma_d),
               inc="dgb", n=16)
        P.emit("s", lambda e: e.dma_start(self.bbc[:], self.beta_d),
               inc="dgb", n=16)
        dgb = P.cnt("dgb")
        used = {0: 0, 1: 0}
        dy_used = {0: 0, 1: 0}
        for tb in range(8):
            par = tb % 2
            P.wait("s", "dpre0", self.DPRE[0])
            P.wait("s", "dpre1", self.DPRE[1])
            P.wait("s", "dx2s0", P.cnt("dx2s0"))
            P.wait("s", "dx2s1", P.cnt("dx2s1"))
            if used[par]:
                P.wait("s", "v_y", used[par])
            P.emit("s", lambda e, tb=tb, mb=mb, par=par: e.dma_start(
                self.pret[par][:],
                self.pre_scr[(mb * 8 + tb) * 128:(mb * 8 + tb + 1) * 128, :]),
                inc=f"dpret{par}", n=16)
            P.emit("s", lambda e, tb=tb, mb=mb, par=par: e.dma_start(
                self.x2f[par][:],
                self.x2_scr[(mb * 8 + tb) * 128:(mb * 8 + tb + 1) * 128, :]),
                inc=f"dpret{par}", n=16)
            dv = P.cnt(f"dpret{par}")
            lnv = self.emit_ln_d5(par, ("dpret" + str(par), dv), dgb, tb,
                                  dy_used[par])
            used[par] = lnv
            P.wait("s", "v_y", lnv)
            P.emit("s", lambda e, tb=tb, mb=mb, par=par: e.dma_start(
                self.y_d[(mb * 8 + tb) * 128:(mb * 8 + tb + 1) * 128, :],
                self.yst[par][:]), inc=f"dy{par}", n=16)
            dy_used[par] = P.cnt(f"dy{par}")

    def emit_ln_d5(self, par, src_ready, dgb, tb, dyfree=0):
        P = self
        nm, v = src_ready
        src = self.pret[par]
        stt = self.stt
        P.wait("a", nm, v)
        P.emit("a", lambda e: e.activation(self.tst[:], src[:], AF.Square),
               inc="a_sq")
        P.wait("v", nm, v)
        P.emit("v", lambda e: e.reduce_sum(stt[:, 0:1], src[:], axis=AX.X))
        P.wait("v", "a_sq", P.cnt("a_sq"))
        P.emit("v", lambda e: e.reduce_sum(stt[:, 1:2], self.tst[:], axis=AX.X))
        P.chain("v")
        P.emit("v", lambda e: e.tensor_scalar(
            stt[:, 2:3], stt[:, 0:1], 1.0 / HID, None, ALU.mult))
        P.emit("v", lambda e: e.tensor_scalar(
            stt[:, 3:4], stt[:, 1:2], 1.0 / HID, None, ALU.mult))
        P.chain("v")
        P.emit("v", lambda e: e.tensor_tensor(
            stt[:, 4:5], stt[:, 2:3], stt[:, 2:3], ALU.mult))
        P.chain("v")
        P.emit("v", lambda e: e.tensor_sub(
            stt[:, 5:6], stt[:, 3:4], stt[:, 4:5]), inc="v_var")
        P.wait("a", "v_var", P.cnt("v_var"))
        P.emit("a", lambda e: e.activation(
            stt[:, 6:7], stt[:, 5:6], AF.Ln, bias=self.epsv[:]), inc="a_ln")
        P.wait("a", "a_ln", P.cnt("a_ln"))
        P.emit("a", lambda e: e.activation(
            stt[:, 7:8], stt[:, 6:7], AF.Exp, scale=-0.5), inc="a_rstd")
        P.wait("v", "a_rstd", P.cnt("a_rstd"))
        P.emit("v", lambda e: e.tensor_scalar(
            self.tst[:], src[:], stt[:, 2:3], stt[:, 7:8],
            ALU.subtract, ALU.mult))
        if tb == 0:
            P.wait("v", "dgb", dgb)
        P.chain("v")
        P.emit("v", lambda e: e.tensor_mul(self.tst[:], self.tst[:], self.gbc[:]))
        P.chain("v")
        P.emit("v", lambda e: e.tensor_add(self.tst[:], self.tst[:], self.bbc[:]))
        if dyfree:
            P.wait("v", f"dy{par}", dyfree)
        P.chain("v")
        P.emit("v", lambda e, par=par: e.tensor_add(
            self.yst[par][:], self.tst[:], self.x2f[par][:]), inc="v_y")
        return P.cnt("v_y")

    def build(self):
        self.declare()
        self.ph_init()
        self.ph_A()
        if PHASES_UPTO >= 1:
            self.barrier("A_B0")
            self.ph_B0()
        if PHASES_UPTO >= 2:
            self.barrier("B0_B1")
            self.ph_B1()
        if PHASES_UPTO >= 3:
            for mb in range(2):
                self.barrier(f"pre_C{mb}")
                self.ph_C(mb)
                if PHASES_UPTO >= 4:
                    self.barrier(f"C_D{mb}")
                    self.ph_D(mb)
                    self.barrier(f"D_D5{mb}")
                    self.ph_D5(mb)
        for nm in ("dy0", "dy1", "ddbg", "dlnst0", "dlnst1", "dqts", "dats",
                   "dln2", "dpre0", "dpre1", "dx2s0", "dx2s1"):
            v = self.cnt(nm)
            if v:
                self.wait("g", nm, v)
        blk = self.ctx.enter_context(self.nc.Block())
        secs = {"t": blk.tensor, "v": blk.vector, "a": blk.scalar,
                "s": blk.sync, "g": blk.gpsimd}
        for key in "tvasg":
            ops = self.ops[key]

            def body(e, ops=ops):
                for fn in ops:
                    fn(e)
            secs[key](body)
        return self.nc


_CACHE = {}


def _get_program(taps=()):
    key = (tuple(sorted(taps)), PHASES_UPTO)
    if key not in _CACHE:
        mk = MK(debug_taps=taps)
        nc = mk.build()
        _CACHE[key] = (nc, mk)
    return _CACHE[key]


def kernel(L0_attn_hidden_states, L0_attn_Wqkv, L0_attn_Wo, L0_ffn_gate_w,
           L0_ffn_up_w, L0_ffn_down_w, L0_ffn_fc1_w, L0_ffn_fc2_w,
           L0_ffn_expert_w, L0_ffn_ln_gamma, L0_ffn_ln_beta):
    x = np.ascontiguousarray(np.asarray(L0_attn_hidden_states, dtype=np.float32))
    w = {
        "wqkv": np.ascontiguousarray(np.asarray(L0_attn_Wqkv, np.float32)),
        "wo": np.ascontiguousarray(np.asarray(L0_attn_Wo, np.float32)),
        "gatew": np.ascontiguousarray(np.asarray(L0_ffn_gate_w, np.float32)),
        "upw": np.ascontiguousarray(np.asarray(L0_ffn_up_w, np.float32)),
        "downw": np.ascontiguousarray(np.asarray(L0_ffn_down_w, np.float32)),
        "fc1w": np.ascontiguousarray(np.asarray(L0_ffn_fc1_w, np.float32)),
        "fc2w": np.ascontiguousarray(np.asarray(L0_ffn_fc2_w, np.float32)),
        "expw": np.ascontiguousarray(np.asarray(L0_ffn_expert_w, np.float32)),
    }
    gamma = np.ascontiguousarray(np.broadcast_to(
        np.asarray(L0_ffn_ln_gamma, np.float32), (128, HID)))
    beta = np.ascontiguousarray(np.broadcast_to(
        np.asarray(L0_ffn_ln_beta, np.float32), (128, HID)))
    gtabs = [_g_table(0), _g_table(1)]
    mtabs = [_mask_table(0), _mask_table(1)]

    nc, _mk = _get_program(DEBUG_TAPS)
    in_maps = []
    for c in range(8):
        b, p = c // 2, c % 2
        xb = x[b]
        x_own = np.ascontiguousarray(np.concatenate(
            [xb[512 * gs:512 * (gs + 1)] for gs in STRIPS[p]], axis=0))
        im = {"x_all": np.ascontiguousarray(xb), "x_own": x_own,
              "gamma_bc": gamma, "beta_bc": beta,
              "gtab": gtabs[p], "masktab": mtabs[p]}
        im.update(w)
        in_maps.append(im)

    res = run_bass_kernel_spmd(nc, in_maps, core_ids=list(range(8)))
    kernel._last_results = res.results

    y = np.empty((B, S, HID), np.float32)
    for c in range(8):
        b, p = c // 2, c % 2
        yo = res.results[c]["y"]
        for ss, gs in enumerate(STRIPS[p]):
            y[b, 512 * gs:512 * (gs + 1)] = yo[512 * ss:512 * (ss + 1)]
    return y


# version 02
# speedup vs baseline: 1.2214x; 1.2214x over previous
import math
from contextlib import ExitStack

import numpy as np
import ml_dtypes

import concourse.bass as bass
import concourse.mybir as mybir
from concourse.bass_utils import run_bass_kernel_spmd

F32 = mybir.dt.float32
BF16 = mybir.dt.bfloat16
AF = mybir.ActivationFunctionType
ALU = mybir.AluOpType
AX = mybir.AxisListType

B, S, HID = 4, 4096, 4096
NH, DH = 32, 128
INT, LAT, NE = 8192, 2048, 8
NHC, NIC, NLC = 32, 64, 16
EPS = 1e-5
SCALE = 1.0 / math.sqrt(DH)
STRIPS = [[0, 3, 4, 7], [1, 2, 5, 6]]
PC = [8, 16, 24, 32]
SLOT_BASE = [0, 8, 24, 48]
TOT_SLOTS = 80

SBUF_BASE = 16512
SBUF_TOP = 229344

PHASES_UPTO = 99
DEBUG_TAPS = ()


def _alibi_slopes():
    n = 32
    cp2 = 2 ** int(math.floor(math.log2(n)))
    base = 2.0 ** (-(2.0 ** (-(math.log2(cp2) - 3))))
    return base ** np.arange(1, cp2 + 1, dtype=np.float64)


def _diag_positions(ss):
    out = []
    for g in (STRIPS[0][ss], STRIPS[1][ss]):
        for m in range(4):
            out.append((4 * g + m, m))
    return out


def _g_table(p):
    sl = _alibi_slopes()
    cols = np.zeros((TOT_SLOTS * 128,), np.float64)
    for ss in range(4):
        gs = STRIPS[p][ss]
        for j in range(PC[ss]):
            slot = SLOT_BASE[ss] + j
            for h in range(32):
                for qb in range(4):
                    d = j - (4 * gs + qb)
                    g = 0.0 if d > 0 else math.exp(sl[h] * 64.0 * d)
                    cols[(slot * 32 + h) * 4 + qb] = g
    tbl = np.broadcast_to(cols.astype(np.float32), (128, TOT_SLOTS * 128))
    return np.ascontiguousarray(tbl)


def _mask_table(p):
    tri = np.tril(np.ones((128, 128), np.float32)).T
    ones = np.ones((128, 128), np.float32)
    blocks = []
    for ss in range(4):
        gs = STRIPS[p][ss]
        for (j, m) in _diag_positions(ss):
            blocks.append(tri if j == 4 * gs + m else ones)
    return np.ascontiguousarray(
        np.concatenate(blocks, axis=1).astype(ml_dtypes.bfloat16))


def _alibi_kk():
    sl = _alibi_slopes()
    kk = np.arange(128, dtype=np.float64) - 64.0
    return np.ascontiguousarray(np.outer(kk, sl).astype(np.float32))


class MK:
    def __init__(self, debug_taps=()):
        self.nc = bass.Bass("TRN2", target_bir_lowering=False)
        self.ctx = ExitStack()
        self.taps = set(debug_taps)
        self.ops = {e: [] for e in "tvasg"}
        self.sems = {}
        self.last_bank = {}
        self.sboff = SBUF_BASE
        self.marks = {}

    def semh(self, name):
        if name not in self.sems:
            h = self.ctx.enter_context(self.nc.semaphore(name))
            self.sems[name] = [h, 0]
        return self.sems[name]

    def cnt(self, name):
        return self.semh(name)[1]

    def wait(self, eng, name, val):
        if val <= 0:
            return
        h = self.semh(name)[0]
        self.ops[eng].append(lambda e, h=h, v=val: e.wait_ge(h, v))

    def emit(self, eng, fn, inc=None, n=1):
        if inc is None:
            self.ops[eng].append(fn)
            return None
        s = self.semh(inc)
        s[1] += n
        if eng in "va":
            self.ops[eng].append(lambda e, fn=fn: fn(e))
            self.ops[eng].append(
                lambda e, h=s[0], n=n: e.drain(fusable=True).then_inc(h, n))
        else:
            self.ops[eng].append(
                lambda e, fn=fn, h=s[0], n=n: fn(e).then_inc(h, n))
        return s[1]

    def drain_inc(self, eng, name, n=1):
        s = self.semh(name)
        s[1] += n
        self.ops[eng].append(
            lambda e, h=s[0], n=n: e.drain(fusable=True).then_inc(h, n))
        return s[1]

    def barrier(self, tag):
        name = "bar_" + tag
        for e in "tvasg":
            self.emit(e, lambda en: en.nop(), inc=name)
        v = self.cnt(name)
        for e in "tvasg":
            self.wait(e, name, v)

    def bank_wait(self, *banks):
        for b in banks:
            if b in self.last_bank:
                nm, v = self.last_bank.pop(b)
                self.wait("t", nm, v)

    def bank_used(self, b, semname):
        self.last_bank[b] = (semname, self.cnt(semname))

    def mark(self, name):
        self.marks[name] = self.sboff

    def reset(self, name):
        self.sboff = self.marks[name]

    def al(self, name, shape, dtype, align=64, at=None):
        if at is None:
            per = int(np.prod(shape[1:])) * mybir.dt.size(dtype)
            self.sboff = (self.sboff + align - 1) // align * align
            at = self.sboff
            self.sboff += per
        assert self.sboff <= SBUF_TOP, f"SBUF overflow at {name}: {self.sboff}"
        return self.nc.alloc_sbuf_tensor_at(name, list(shape), dtype, offset=at)

    def din(self, name, shape, dtype=F32):
        return self.nc.dram_tensor(name, list(shape), dtype,
                                   kind="ExternalInput").ap()

    def dscr(self, name, shape, dtype):
        kind = "ExternalOutput" if name in self.taps else "Internal"
        return self.nc.dram_tensor(name, list(shape), dtype, kind=kind).ap()

    def declare(self):
        nc = self.nc
        self.x_all = self.din("x_all", [S, HID])
        self.x_own = self.din("x_own", [2048, HID])
        self.wqkv = self.din("wqkv", [HID, 4352])
        self.wo_d = self.din("wo", [HID, HID])
        self.gatew_d = self.din("gatew", [HID, NE])
        self.upw_d = self.din("upw", [HID, INT])
        self.downw_d = self.din("downw", [INT, HID])
        self.fc1w_d = self.din("fc1w", [HID, LAT])
        self.fc2w_d = self.din("fc2w", [LAT, HID])
        self.expw_d = self.din("expw", [LAT, LAT])
        self.gamma_d = self.din("gamma_bc", [128, HID])
        self.beta_d = self.din("beta_bc", [128, HID])
        self.gtab_d = self.din("gtab", [128, TOT_SLOTS * 128])
        self.mask_d = self.din("masktab", [128, 32 * 128], BF16)

        self.y_d = nc.dram_tensor("y", [2048, HID], F32,
                                  kind="ExternalOutput").ap()

        self.ln1_scr = self.dscr("ln1_scr", [S, HID], BF16)
        self.ln1o_scr = self.dscr("ln1o_scr", [2048, HID], BF16)
        self.qt_scr = self.dscr("qt_scr", [4, 128, NH * 512], BF16)
        self.at_scr = self.dscr("at_scr", [4, 128, NHC * 512], BF16)
        self.x2_scr = self.dscr("x2_scr", [2048, HID], F32)
        self.ln2_scr = self.dscr("ln2_scr", [2048, HID], BF16)
        self.pre_scr = self.dscr("pre_scr", [2048, HID], F32)
        self.rw_scr = self.dscr("rw_scr", [1, 1024], F32)
        self.den_scr = self.dscr("den_scr", [32, 512], F32)
        self.kt2_scr = self.dscr("kt2_scr", [128, S], F32)
        self.s0_scr = self.dscr("s0_scr", [128, 512], F32)
        self.pt_scr = self.dscr("pt_scr", [8, 128, 512], F32)
        self.pe_scr = self.dscr("pe_scr", [8, 128, 512], F32)
        self.kv_scr = self.dscr("kv_scr", [2, 128, S], F32)
        self.rwdbg_scr = self.dscr("rwdbg_scr", [128, 16], F32)

        self.onesb_d = nc.inline_tensor(
            np.ones((128, 1), ml_dtypes.bfloat16), "onesb_cst")
        self.onesf_d = nc.inline_tensor(np.ones((1, 128), np.float32), "onesf_cst")
        self.identf_d = nc.inline_tensor(np.eye(128, dtype=np.float32), "identf_cst")
        self.alibi_d = nc.inline_tensor(_alibi_kk(), "alibi_cst")
        self.epsv_d = nc.inline_tensor(np.full((128, 1), EPS, np.float32), "eps_cst")

        al = self.al
        self.KT = al("KT", [128, S], BF16)
        self.VN = al("VN", [128, S], BF16)
        self.onesb = al("onesb", [128, 1], BF16)
        self.onesf = al("onesf", [1, 128], F32)
        self.identf = al("identf", [128, 128], F32)
        self.alibi = al("alibi", [128, 32], F32)
        self.epsv = al("epsv", [128, 1], F32)
        self.gatew = al("gatew_sb", [128, NHC, 8], BF16)
        self.stt = al("stt", [128, 16], F32)
        self.st2 = al("st2", [128, 64], F32)
        self.rw_nat = al("rw_nat", [128, 8], F32)
        self.rwT = al("rwT", [8, 128], F32)
        self.rw_row = al("rw_row", [1, 1024], F32)
        self.recip = al("recip", [1, 1024], F32)
        self.rb_sb = [al(f"rb_sb{i}", [128, 512], F32) for i in range(2)]
        self.flushb = al("flushb", [1, 512], F32)
        self.denst = al("denst", [1, 512], F32)
        self.dmpst = al("dmpst", [128, 512], F32)
        self.rw_bc = al("rw_bc", [128, 1024], F32)
        self.mark("shared")

        self.xa = [al(f"xa{i}", [128, HID], F32) for i in range(2)]
        self.sqst = al("sqst", [128, HID], F32)
        self.lnst = [al(f"lnst{i}", [128, HID], BF16) for i in range(2)]
        self.l1t = al("l1t", [128, NHC * 512], BF16)
        self.wkv = al("wkv", [128, NHC, 256], BF16)

        self.reset("shared")
        self.l1ot = al("l1ot", [128, NHC * 512], BF16)
        self.wq = [al(f"wq{i}", [128, NHC, 128], BF16) for i in range(2)]
        self.qT0 = al("qT0", [128, NH * 512], BF16)

        self.reset("shared")
        self.qt = al("qt", [128, NH * 512], BF16)
        self.attnT = al("attnT", [128, NHC * 512], BF16)
        self.masks = al("masks", [128, 32, 128], BF16)
        self.biasb = al("biasb", [128, 4096], F32)
        self.peh = [al(f"peh{i}", [128, 512], BF16) for i in range(2)]
        self.ptb = [al(f"ptb{i}", [128, 512], BF16) for i in range(3)]

        self.reset("shared")
        self.amac = al("amac", [128, NHC, 2, 512], BF16)
        self.wob = [al(f"wob{i}", [128, 4, 512], BF16) for i in range(2)]
        self.xsl = [al(f"xsl{i}", [128, 512], F32) for i in range(2)]
        self.x2st = [al(f"x2st{i}", [128, 512], F32) for i in range(2)]
        self.x2t = al("x2t", [128, HID], F32)
        self.sqln = al("sqln", [128, HID], F32)
        self.ln2o = al("ln2o", [128, HID], BF16)

        self.reset("shared")
        self.l2t = al("l2t", [128, NHC * 1024], BF16)
        self.hTq = al("hTq", [128, 16 * 1024], BF16)
        self.latT = al("latT", [128, NLC * 1024], BF16,
                       at=self.nc.lookup_mloc(self.hTq).addr)
        self.moeT = al("moeT", [128, NLC * 1024], BF16)
        self.wfb = [al(f"wfb{i}", [128, NHC, 128], BF16) for i in range(2)]
        self.fc2b = al("fc2b", [128, NLC, 512], BF16,
                       at=self.nc.lookup_mloc(self.wfb[0]).addr)
        self.dwb = [al(f"dwb{i}", [128, 8, 512], BF16) for i in range(2)]
        self.relust = [al(f"relust{i}", [128, 512], BF16) for i in range(2)]
        self.prest = [al(f"prest{i}", [128, 512], F32) for i in range(2)]

        self.reset("shared")
        self.pret = [al(f"pret{i}", [128, HID], F32) for i in range(2)]
        self.x2f = [al(f"x2f{i}", [128, HID], F32) for i in range(2)]
        self.gbc = al("gbc", [128, HID], F32)
        self.bbc = al("bbc", [128, HID], F32)
        self.tst = al("tst", [128, HID], F32)
        self.yst = [al(f"yst{i}", [128, HID], F32) for i in range(2)]

        self.pb = [self.ctx.enter_context(
            self.nc.psum_tensor(f"pb{i}", [128, 512], F32)) for i in range(8)]

    def chain(self, eng, name=None):
        name = name or ("ch_" + eng)
        v = self.drain_inc(eng, name)
        self.wait(eng, name, v)

    def emit_ln(self, src, sqbuf, dst, src_ready, dst_free=None):
        P = self
        nm, v = src_ready
        stt = self.stt
        P.wait("a", nm, v)
        P.emit("a", lambda e, src=src, sqbuf=sqbuf: e.activation(
            sqbuf[:], src[:], AF.Square), inc="a_sq")
        asq = P.cnt("a_sq")
        P.wait("v", nm, v)
        P.emit("v", lambda e, src=src: e.reduce_sum(
            stt[:, 0:1], src[:], axis=AX.X))
        P.wait("v", "a_sq", asq)
        P.emit("v", lambda e, sqbuf=sqbuf: e.reduce_sum(
            stt[:, 1:2], sqbuf[:], axis=AX.X))
        P.chain("v")
        P.emit("v", lambda e: e.tensor_scalar(
            stt[:, 2:3], stt[:, 0:1], 1.0 / HID, None, ALU.mult))
        P.emit("v", lambda e: e.tensor_scalar(
            stt[:, 3:4], stt[:, 1:2], 1.0 / HID, None, ALU.mult))
        P.chain("v")
        P.emit("v", lambda e: e.tensor_tensor(
            stt[:, 4:5], stt[:, 2:3], stt[:, 2:3], ALU.mult))
        P.chain("v")
        P.emit("v", lambda e: e.tensor_sub(
            stt[:, 5:6], stt[:, 3:4], stt[:, 4:5]), inc="v_var")
        P.wait("a", "v_var", P.cnt("v_var"))
        P.emit("a", lambda e: e.activation(
            stt[:, 6:7], stt[:, 5:6], AF.Ln, bias=self.epsv[:]), inc="a_ln")
        P.wait("a", "a_ln", P.cnt("a_ln"))
        P.emit("a", lambda e: e.activation(
            stt[:, 7:8], stt[:, 6:7], AF.Exp, scale=-0.5), inc="a_rstd")
        ar = P.cnt("a_rstd")
        if dst_free is not None and dst_free[1] > 0:
            P.wait("v", dst_free[0], dst_free[1])
        P.wait("v", "a_rstd", ar)
        P.emit("v", lambda e, src=src, dst=dst: e.tensor_scalar(
            dst[:], src[:], stt[:, 2:3], stt[:, 7:8],
            ALU.subtract, ALU.mult), inc="v_ln")
        return P.cnt("v_ln")

    def ph_init(self):
        P = self
        for buf, dr in ((self.onesb, self.onesb_d), (self.onesf, self.onesf_d),
                        (self.identf, self.identf_d), (self.alibi, self.alibi_d),
                        (self.epsv, self.epsv_d)):
            P.emit("s", lambda e, b=buf, d=dr: e.dma_start(b[:], d.ap()),
                   inc="dcst", n=16)
        self.CST = P.cnt("dcst")
        P.emit("g", lambda e: e.dma_start(
            self.wkv[:], self.wqkv[:, 4096:4352]
            .rearrange("(c p) j -> p c j", p=128)), inc="dwkv", n=16)
        for hc in range(NHC):
            P.emit("g", lambda e, hc=hc: e.dma_start(
                self.gatew[:, hc, :],
                self.gatew_d[hc * 128:(hc + 1) * 128, :]), inc="dgate", n=16)
        self.GATEW = P.cnt("dgate")

    def ph_A(self):
        P = self
        for e in "tvasg":
            P.wait(e, "dcst", self.CST)
            P.wait(e, "dwkv", P.cnt("dwkv"))
            P.wait(e, "dgate", self.GATEW)
        lnused = {0: 0, 1: 0}
        lnst_stored = {0: 0, 1: 0}
        kvdone = 0

        def ln_tile(src_rows_ap, dst_rows_ap, par):
            P.wait("s", "v_ln_used", lnused[par])
            P.emit("s", lambda e: e.dma_start(self.xa[par][:], src_rows_ap),
                   inc=f"dxa{par}", n=16)
            dv = P.cnt(f"dxa{par}")
            lnv = self.emit_ln(self.xa[par], self.sqst, self.lnst[par],
                               (f"dxa{par}", dv),
                               dst_free=(f"dlnst{par}", lnst_stored[par]))
            lnused[par] = P.drain_inc("v", "v_ln_used")
            P.wait("s", "v_ln", lnv)
            P.emit("s", lambda e: e.dma_start(dst_rows_ap, self.lnst[par][:]),
                   inc=f"dlnst{par}", n=16)
            lnst_stored[par] = P.cnt(f"dlnst{par}")

        for gs in range(8):
            for t4 in range(4):
                gt = gs * 4 + t4
                ln_tile(self.x_all[gt * 128:(gt + 1) * 128, :],
                        self.ln1_scr[gt * 128:(gt + 1) * 128, :], gt % 2)
                if gt == 0 and "rwdbg_scr" in self.taps:
                    P.wait("s", "v_ln", P.cnt("v_ln"))
                    P.emit("s", lambda e: e.dma_start(
                        self.rwdbg_scr[:], self.stt[:]), inc="ddbg", n=16)
                    P.wait("s", "ddbg", P.cnt("ddbg"))
                    P.wait("v", "ddbg", P.cnt("ddbg"))
                    P.wait("a", "ddbg", P.cnt("ddbg"))
            P.wait("s", f"dlnst0", lnst_stored[0])
            P.wait("s", f"dlnst1", lnst_stored[1])
            if gs > 0:
                P.wait("s", "pe_v", kvdone)
            for hc in range(NHC):
                P.emit("s", lambda e, gs=gs, hc=hc: e.dma_start(
                    self.l1t[:, hc * 512:(hc + 1) * 512],
                    self.ln1_scr[gs * 512:(gs + 1) * 512,
                                 hc * 128:(hc + 1) * 128],
                    transpose=True), inc="dl1t", n=16)
            dl1t = P.cnt("dl1t")
            P.wait("t", "dwkv", 16)
            P.wait("t", "dl1t", dl1t)
            self.bank_wait(0)
            for hc in range(NHC):
                P.emit("t", lambda e, hc=hc: e.matmul(
                    self.pb[0][:], self.wkv[:, hc, 0:128],
                    self.l1t[:, hc * 512:(hc + 1) * 512],
                    start=(hc == 0), stop=(hc == NHC - 1)),
                    inc="pe_k" if hc == NHC - 1 else None)
            P.wait("a", "pe_k", P.cnt("pe_k"))
            P.emit("a", lambda e, gs=gs: e.activation(
                self.KT[:, gs * 512:(gs + 1) * 512], self.pb[0][:],
                AF.Copy, scale=SCALE), inc="a_kev")
            self.bank_used(0, "a_kev")
            for t4 in range(4):
                self.bank_wait(1)
                for hc in range(NHC):
                    P.emit("t", lambda e, hc=hc, t4=t4: e.matmul(
                        self.pb[1][:, 0:128],
                        self.l1t[:, hc * 512 + t4 * 128:
                                 hc * 512 + (t4 + 1) * 128],
                        self.wkv[:, hc, 128:256],
                        start=(hc == 0), stop=(hc == NHC - 1)),
                        inc="pe_v" if hc == NHC - 1 else None)
                P.wait("v", "pe_v", P.cnt("pe_v"))
                P.emit("v", lambda e, gs=gs, t4=t4: e.tensor_copy(
                    self.VN[:, (gs * 4 + t4) * 128:(gs * 4 + t4 + 1) * 128],
                    self.pb[1][:, 0:128]), inc="v_vev")
                self.bank_used(1, "v_vev")
            kvdone = P.cnt("pe_v")
            KVSEM = "pe_v"

        for ot in range(16):
            ln_tile(self.x_own[ot * 128:(ot + 1) * 128, :],
                    self.ln1o_scr[ot * 128:(ot + 1) * 128, :], ot % 2)
        self.DLNO = P.cnt("dlnst0") + P.cnt("dlnst1")

        self.LNST0 = P.cnt("dlnst0")
        self.LNST1 = P.cnt("dlnst1")

        if "kv_scr" in self.taps:
            P.wait("v", "a_kev", P.cnt("a_kev"))
            P.emit("v", lambda e: e.tensor_copy(self.sqst[:, 0:4096],
                                                self.KT[:]), inc="v_dbg")
            P.wait("s", "v_dbg", P.cnt("v_dbg"))
            P.emit("s", lambda e: e.dma_start(self.kv_scr[0], self.sqst[:, 0:4096]),
                   inc="ddbg", n=16)
            P.wait("v", "ddbg", P.cnt("ddbg"))
            P.emit("v", lambda e: e.tensor_copy(self.sqst[:, 0:4096],
                                                self.KT[:]), inc="v_dbg")
            P.wait("s", "v_dbg", P.cnt("v_dbg"))
            P.emit("s", lambda e: e.dma_start(self.kv_scr[1], self.sqst[:, 0:4096]),
                   inc="ddbg", n=16)
            P.wait("s", "ddbg", P.cnt("ddbg"))

    def ph_B0(self):
        P = self
        qbdone = 0
        qmm = {0: 0, 1: 0}
        for ss in range(4):
            if ss > 0:
                P.wait("s", "pe_qmm", qbdone)
                P.wait("a", "dqts", P.cnt("dqts"))
            P.wait("s", "dlnst0", self.LNST0)
            P.wait("s", "dlnst1", self.LNST1)
            for hc in range(NHC):
                P.emit("s", lambda e, ss=ss, hc=hc: e.dma_start(
                    self.l1ot[:, hc * 512:(hc + 1) * 512],
                    self.ln1o_scr[ss * 512:(ss + 1) * 512,
                                  hc * 128:(hc + 1) * 128],
                    transpose=True), inc="dl1ot", n=16)
            dl1ot = P.cnt("dl1ot")
            for oc in range(NH):
                par = oc % 2
                if qmm[par] == 0:
                    P.wait("g", "dlnst0", self.LNST0)
                    P.wait("g", "dlnst1", self.LNST1)
                P.wait("g", "pe_qmm", qmm[par])
                P.emit("g", lambda e, oc=oc, par=par: e.dma_start(
                    self.wq[par][:],
                    self.wqkv[:, oc * 128:(oc + 1) * 128]
                    .rearrange("(c p) j -> p c j", p=128)),
                    inc=f"dwq{par}", n=16)
                P.wait("t", f"dwq{par}", P.cnt(f"dwq{par}"))
                P.wait("t", "dl1ot", dl1ot)
                self.bank_wait(par)
                for hc in range(NHC):
                    P.emit("t", lambda e, hc=hc, par=par: e.matmul(
                        self.pb[par][:], self.wq[par][:, hc, :],
                        self.l1ot[:, hc * 512:(hc + 1) * 512],
                        start=(hc == 0), stop=(hc == NHC - 1)),
                        inc="pe_qmm" if hc == NHC - 1 else None)
                qmm[par] = P.cnt("pe_qmm")
                P.wait("a", "pe_qmm", qmm[par])
                P.emit("a", lambda e, oc=oc, par=par: e.activation(
                    self.qT0[:, oc * 512:(oc + 1) * 512], self.pb[par][:],
                    AF.Copy), inc="a_qev")
                self.bank_used(par, "a_qev")
            P.wait("a", "a_qev", P.cnt("a_qev"))
            P.drain_inc("a", "a_qfl")
            P.wait("s", "a_qfl", P.cnt("a_qfl"))
            P.emit("s", lambda e, ss=ss: e.dma_start(
                self.qt_scr[ss], self.qT0[:]), inc="dqts", n=16)
            qbdone = P.cnt("pe_qmm")
        self.DQTS = P.cnt("dqts")

    def ph_B1(self):
        P = self
        P.wait("t", "dcst", self.CST)
        P.wait("s", "dqts", self.DQTS)
        P.emit("s", lambda e: e.dma_start(self.masks[:].rearrange(
            "p a b -> p (a b)"), self.mask_d), inc="dmask", n=16)
        P.wait("v", "dmask", P.cnt("dmask"))
        if "kt2_scr" in self.taps:
            for ch in range(8):
                P.wait("v", "dkt2", P.cnt("dkt2"))
                P.emit("v", lambda e, ch=ch: e.tensor_copy(
                    self.dmpst[:], self.KT[:, ch * 512:(ch + 1) * 512]))
                P.emit("v", lambda e: e.tensor_copy(
                    self.stt[:, 13:14], self.dmpst[:, 511:512]), inc="v_kt2")
                P.wait("s", "v_kt2", P.cnt("v_kt2"))
                P.emit("s", lambda e, ch=ch: e.dma_start(
                    self.kt2_scr[:, ch * 512:(ch + 1) * 512], self.dmpst[:]),
                    inc="dkt2", n=16)
        exp_at_sbank = {2: 0, 3: 0}
        vp_of = {}
        pt_free = {0: 0, 1: 0, 2: 0}
        peh_free = {0: 0, 1: 0}
        avn_free = {0: 0, 1: 0}
        den_free = {0: 0, 1: 0}
        rb_free = {0: 0, 1: 0}
        rbmm_done = {0: 0, 1: 0}
        smm_all = 0
        dens = {}

        def emit_tail(h):
            hp = h % 2
            ss0 = self._cur_ss == 0
            avb = 4 if hp == 0 else 6
            deb = 5 if hp == 0 else 7
            rbb = 0 if hp == 0 else 1
            if ss0 and "den_scr" in self.taps:
                P.wait("a", "pe_ptc", dens[h])
                deb_ = 5 if hp == 0 else 7
                P.emit("a", lambda e, deb_=deb_: e.activation(
                    self.denst[:], self.pb[deb_][0:1, :], AF.Copy))
                P.emit("a", lambda e: e.activation(
                    self.stt[0:1, 12:13], self.denst[0:1, 511:512], AF.Copy),
                    inc="a_dden")
                P.wait("s", "a_dden", P.cnt("a_dden"))
                P.emit("s", lambda e, h=h: e.dma_start(
                    self.den_scr[h:h + 1, :], self.denst[:]), inc="ddump2",
                    n=16)
                P.wait("v", "ddump2", P.cnt("ddump2"))
            P.wait("v", "pe_ptc", dens[h])
            if rb_free[hp]:
                P.wait("v", "pe_rb", rbmm_done[hp])
            P.emit("v", lambda e, deb=deb, hp=hp: e.reciprocal(
                self.recip[0:1, hp * 512:(hp + 1) * 512],
                self.pb[deb][0:1, :]), inc="v_recip")
            den_free[hp] = P.cnt("v_recip")
            self.bank_used(deb, "v_recip")
            P.wait("t", "v_recip", P.cnt("v_recip"))
            self.bank_wait(rbb)
            P.emit("t", lambda e, rbb=rbb, hp=hp: e.matmul(
                self.pb[rbb][:], self.onesf[:],
                self.recip[0:1, hp * 512:(hp + 1) * 512],
                start=True, stop=True), inc="pe_rb")
            rbmm_done[hp] = P.cnt("pe_rb")
            P.wait("a", "pe_rb", P.cnt("pe_rb"))
            P.emit("a", lambda e, rbb=rbb, hp=hp: e.activation(
                self.rb_sb[hp][:], self.pb[rbb][:], AF.Copy), inc="a_rbev")
            rb_free[hp] = P.cnt("a_rbev")
            self.bank_used(rbb, "a_rbev")
            P.wait("v", "a_rbev", rb_free[hp])
            P.emit("v", lambda e, avb=avb, h=h, hp=hp: e.tensor_tensor(
                self.attnT[:, h * 512:(h + 1) * 512], self.pb[avb][:],
                self.rb_sb[hp][:], ALU.mult), inc="v_norm")
            avn_free[hp] = P.cnt("v_norm")
            self.bank_used(avb, "v_norm")

        for ss in range(4):
            self._cur_ss = ss
            npc = PC[ss]
            dpos = _diag_positions(ss)
            if ss > 0:
                P.wait("s", "pe_ptc", smm_all)
                P.wait("s", "dats", P.cnt("dats"))
            P.wait("s", "dqts", self.DQTS)
            P.emit("s", lambda e, ss=ss: e.dma_start(
                self.qt[:], self.qt_scr[ss]), inc="dqt", n=16)
            dqt = P.cnt("dqt")
            c0 = SLOT_BASE[ss] * 128
            c1 = (SLOT_BASE[ss] + npc) * 128
            P.emit("s", lambda e, c0=c0, c1=c1: e.dma_start(
                self.biasb[:, 0:c1 - c0], self.gtab_d[:, c0:c1]),
                inc="dbias", n=16)
            dbias = P.cnt("dbias")
            P.wait("v", "dats", P.cnt("dats"))
            P.wait("v", "dbias", dbias)
            for h in range(NH):
                hp = h % 2
                avb = 4 if hp == 0 else 6
                deb = 5 if hp == 0 else 7
                if avn_free[hp]:
                    P.wait("t", "v_norm", avn_free[hp])
                if den_free[hp]:
                    P.wait("t", "v_recip", den_free[hp])
                self.last_bank.pop(avb, None)
                self.last_bank.pop(deb, None)
                pend = None

                def emit_denav(j, pti):
                    P.wait("t", "v_p", vp_of[(h, j)])
                    P.emit("t", lambda e, deb=deb, pti=pti, j=j, npc=npc:
                           e.matmul(self.pb[deb][0:1, :], self.onesb[:],
                                    self.ptb[pti][:],
                                    start=(j == 0), stop=(j == npc - 1)))
                    P.emit("t", lambda e, avb=avb, pti=pti, j=j, npc=npc:
                           e.matmul(self.pb[avb][:],
                                    self.VN[:, j * 128:(j + 1) * 128],
                                    self.ptb[pti][:],
                                    start=(j == 0), stop=(j == npc - 1)),
                           inc="pe_ptc")
                    pt_free[pti] = P.cnt("pe_ptc")

                for j in range(npc):
                    sbk = 2 + (j % 2)
                    pti = j % 3
                    if j < 2 and h == 0:
                        P.wait("t", "dqt", dqt)
                    if exp_at_sbank[sbk]:
                        P.wait("t", "a_exp", exp_at_sbank[sbk])
                    P.emit("t", lambda e, j=j, h=h, sbk=sbk: e.matmul(
                        self.pb[sbk][:], self.KT[:, j * 128:(j + 1) * 128],
                        self.qt[:, h * 512:(h + 1) * 512],
                        start=True, stop=True), inc="pe_s")
                    P.wait("a", "pe_s", P.cnt("pe_s"))
                    if ss == 0 and j == 0 and h == 0:
                        P.wait("a", "dcst", self.CST)
                    pp = j % 2
                    if peh_free[pp]:
                        P.wait("a", "v_p", peh_free[pp])
                    P.emit("a", lambda e, sbk=sbk, h=h, pp=pp: e.activation(
                        self.peh[pp][:], self.pb[sbk][:], AF.Exp,
                        bias=self.alibi[:, h:h + 1]), inc="a_exp")
                    exp_at_sbank[sbk] = P.cnt("a_exp")
                    P.wait("v", "a_exp", exp_at_sbank[sbk])
                    if ss == 0 and j == 0 and h == 0:
                        P.wait("v", "dcst", self.CST)
                    if pt_free[pti]:
                        P.wait("v", "pe_ptc", pt_free[pti])
                    for qb in range(4):
                        gc = (j * 32 + h) * 4 + qb
                        P.emit("v", lambda e, pti=pti, pp=pp, qb=qb, gc=gc:
                               e.tensor_scalar(
                                   self.ptb[pti][:, qb * 128:(qb + 1) * 128],
                                   self.peh[pp][:, qb * 128:(qb + 1) * 128],
                                   self.biasb[:, gc:gc + 1],
                                   self.biasb[:, gc:gc + 1],
                                   ALU.mult, ALU.mult))
                    if any(dj == j for dj, _ in dpos):
                        P.chain("v")
                    for i, (dj, dm) in enumerate(dpos):
                        if dj == j:
                            P.emit("v", lambda e, pti=pti, dm=dm, ss=ss, i=i:
                                   e.tensor_tensor(
                                       self.ptb[pti][:, dm * 128:(dm + 1) * 128],
                                       self.ptb[pti][:, dm * 128:(dm + 1) * 128],
                                       self.masks[:, ss * 8 + i, :], ALU.mult))
                    peh_free[pp] = P.drain_inc("v", "v_p")
                    vp_of[(h, j)] = P.cnt("v_p")
                    if pend is not None:
                        emit_denav(*pend)
                    pend = (j, pti)
                emit_denav(*pend)
                dens[h] = P.cnt("pe_ptc")
                if h > 0:
                    emit_tail(h - 1)
            emit_tail(NH - 1)
            smm_all = P.cnt("pe_ptc")
            P.wait("v", "v_norm", P.cnt("v_norm"))
            P.drain_inc("v", "v_atfl")
            P.wait("s", "v_atfl", P.cnt("v_atfl"))
            P.emit("s", lambda e, ss=ss: e.dma_start(
                self.at_scr[ss], self.attnT[:]), inc="dats", n=16)
        self.DATS = P.cnt("dats")

    def ph_C(self, mb):
        P = self
        P.wait("s", "dats", self.DATS)
        for k in range(2):
            ss = 2 * mb + k
            P.emit("s", lambda e, ss=ss, k=k: e.dma_start(
                self.amac[:, :, k, :],
                self.at_scr[ss].rearrange("p (c j) -> p c j", j=512)),
                inc="damac", n=16)
        damac = P.cnt("damac")
        P.wait("t", "damac", damac)
        if not hasattr(self, "_wo_gchunk"):
            self._wo_gchunk = 0
        xsl_used = {0: 0, 1: 0}
        x2st_stored = {0: 0, 1: 0}
        for ocs in range(8):
            dwo_val = []
            for c4 in range(8):
                par = c4 % 2
                g = self._wo_gchunk
                self._wo_gchunk += 1
                P.wait("g", "pe_woc", g - 1)
                P.emit("g", lambda e, c4=c4, ocs=ocs, par=par: e.dma_start(
                    self.wob[par][:],
                    self.wo_d[c4 * 512:(c4 + 1) * 512,
                              ocs * 512:(ocs + 1) * 512]
                    .rearrange("(c p) j -> p c j", p=128)),
                    inc=f"dwo{par}", n=16)
                dwo_val.append(P.cnt(f"dwo{par}"))
            self.bank_wait(*range(8))
            for hc in range(NHC):
                c4 = hc // 4
                par = c4 % 2
                if hc % 4 == 0:
                    P.wait("t", f"dwo{par}", dwo_val[c4])
                for tb in range(8):
                    inc = None
                    if tb == 7 and hc % 4 == 3:
                        inc = "pe_woc"
                    P.emit("t", lambda e, hc=hc, tb=tb, par=par: e.matmul(
                        self.pb[tb][:], self.amac[:, hc, tb // 4,
                                                  (tb % 4) * 128:
                                                  (tb % 4 + 1) * 128],
                        self.wob[par][:, hc % 4, :],
                        start=(hc == 0), stop=(hc == NHC - 1)),
                        inc=inc)
            pewo_woc = P.cnt("pe_woc")
            for tb in range(8):
                par = tb % 2
                P.wait("s", "v_x2", xsl_used[par])
                P.emit("s", lambda e, tb=tb, ocs=ocs, mb=mb, par=par:
                       e.dma_start(self.xsl[par][:],
                                   self.x_own[(mb * 8 + tb) * 128:
                                              (mb * 8 + tb + 1) * 128,
                                              ocs * 512:(ocs + 1) * 512]),
                       inc=f"dxsl{par}", n=16)
                P.wait("v", f"dxsl{par}", P.cnt(f"dxsl{par}"))
                P.wait("v", "pe_woc", pewo_woc)
                P.wait("v", f"dx2s{par}", x2st_stored[par])
                P.emit("v", lambda e, tb=tb, par=par: e.tensor_add(
                    self.x2st[par][:], self.pb[tb][:], self.xsl[par][:]),
                    inc="v_x2")
                xsl_used[par] = P.cnt("v_x2")
                self.bank_used(tb, "v_x2")
                P.wait("s", "v_x2", P.cnt("v_x2"))
                P.emit("s", lambda e, tb=tb, ocs=ocs, mb=mb, par=par:
                       e.dma_start(self.x2_scr[(mb * 8 + tb) * 128:
                                               (mb * 8 + tb + 1) * 128,
                                               ocs * 512:(ocs + 1) * 512],
                                   self.x2st[par][:]), inc=f"dx2s{par}", n=16)
                x2st_stored[par] = P.cnt(f"dx2s{par}")
        P.wait("v", "dy0", P.cnt("dy0"))
        P.wait("v", "dy1", P.cnt("dy1"))
        for tb in range(8):
            P.wait("s", "dx2s0", P.cnt("dx2s0"))
            P.wait("s", "dx2s1", P.cnt("dx2s1"))
            if tb > 0:
                P.wait("s", "v_ln_used", P.cnt("v_ln_used"))
            P.emit("s", lambda e, tb=tb, mb=mb: e.dma_start(
                self.x2t[:], self.x2_scr[(mb * 8 + tb) * 128:
                                         (mb * 8 + tb + 1) * 128, :]),
                inc="dx2t", n=16)
            lnv = self.emit_ln(self.x2t, self.sqln, self.ln2o,
                               ("dx2t", P.cnt("dx2t")),
                               dst_free=("dln2", P.cnt("dln2")))
            P.drain_inc("v", "v_ln_used")
            P.wait("s", "v_ln", lnv)
            P.emit("s", lambda e, tb=tb, mb=mb: e.dma_start(
                self.ln2_scr[(mb * 8 + tb) * 128:(mb * 8 + tb + 1) * 128, :],
                self.ln2o[:]), inc="dln2", n=16)
        self.DLN2 = P.cnt("dln2")

    def ph_D(self, mb):
        P = self
        P.wait("s", "dln2", self.DLN2)
        for hc in range(NHC):
            P.emit("s", lambda e, hc=hc, mb=mb: e.dma_start(
                self.l2t[:, hc * 1024:(hc + 1) * 1024],
                self.ln2_scr[mb * 1024:(mb + 1) * 1024,
                             hc * 128:(hc + 1) * 128],
                transpose=True), inc="dl2t", n=16)
        dl2t = P.cnt("dl2t")

        P.wait("t", "dl2t", dl2t)
        P.wait("t", "dgate", self.GATEW)
        for tb in range(8):
            self.bank_wait(7)
            for hc in range(NHC):
                P.emit("t", lambda e, hc=hc, tb=tb: e.matmul(
                    self.pb[7][:, 0:8],
                    self.l2t[:, hc * 1024 + tb * 128:hc * 1024 + (tb + 1) * 128],
                    self.gatew[:, hc, :],
                    start=(hc == 0), stop=(hc == NHC - 1)),
                    inc="pe_g" if hc == NHC - 1 else None)
            P.wait("v", "pe_g", P.cnt("pe_g"))
            self.emit_top2(tb)
            self.bank_used(7, "v_rw")
        P.wait("t", "v_rw", P.cnt("v_rw"))
        self.bank_wait(6)
        P.emit("t", lambda e: e.transpose(
            self.pb[6][0:8, 0:128], self.rw_nat[:], self.identf[:]),
            inc="pe_rwt")
        P.wait("a", "pe_rwt", P.cnt("pe_rwt"))
        P.emit("a", lambda e: e.activation(
            self.rwT[:], self.pb[6][0:8, 0:128], AF.Copy), inc="a_rwt")
        self.bank_used(6, "a_rwt")
        P.wait("s", "a_rwt", P.cnt("a_rwt"))
        P.emit("s", lambda e: e.dma_start(
            self.rw_scr.rearrange("o (a b) -> (o a) b", b=128), self.rwT[:]),
            inc="drws", n=16)
        P.wait("s", "drws", P.cnt("drws"))
        P.emit("s", lambda e: e.dma_start(self.rw_row[:], self.rw_scr),
               inc="drwl", n=16)
        P.wait("t", "drwl", P.cnt("drwl"))
        for half in range(2):
            self.bank_wait(6)
            P.emit("t", lambda e, half=half: e.matmul(
                self.pb[6][:], self.onesf[:],
                self.rw_row[0:1, half * 512:(half + 1) * 512],
                start=True, stop=True), inc="pe_rwb")
            P.wait("a", "pe_rwb", P.cnt("pe_rwb"))
            P.emit("a", lambda e, half=half: e.activation(
                self.rw_bc[:, half * 512:(half + 1) * 512],
                self.pb[6][:], AF.Copy), inc="a_rwb")
            self.bank_used(6, "a_rwb")
        ARWB = P.cnt("a_rwb")
        if "rwdbg_scr" in self.taps and mb == 0:
            P.wait("s", "a_rwb", ARWB)
            P.emit("s", lambda e: e.dma_start(
                self.rwdbg_scr[:, 0:8], self.rw_nat[:]), inc="ddbg", n=16)

        f1_used = {0: 0, 1: 0}
        for lc in range(NLC):
            par = lc % 2
            P.wait("g", "pe_f1", f1_used[par])
            P.emit("g", lambda e, lc=lc, par=par: e.dma_start(
                self.wfb[par][:],
                self.fc1w_d[:, lc * 128:(lc + 1) * 128]
                .rearrange("(c p) j -> p c j", p=128)), inc=f"dwf{par}", n=16)
            P.wait("t", f"dwf{par}", P.cnt(f"dwf{par}"))
            self.bank_wait(2 * par, 2 * par + 1)
            for hc in range(NHC):
                for half in range(2):
                    P.emit("t", lambda e, hc=hc, half=half, par=par: e.matmul(
                        self.pb[2 * par + half][:], self.wfb[par][:, hc, :],
                        self.l2t[:, hc * 1024 + half * 512:
                                 hc * 1024 + (half + 1) * 512],
                        start=(hc == 0), stop=(hc == NHC - 1)),
                        inc="pe_f1" if (hc == NHC - 1 and half == 1) else None)
            f1_used[par] = P.cnt("pe_f1")
            P.wait("a", "pe_f1", P.cnt("pe_f1"))
            for half in range(2):
                P.emit("a", lambda e, lc=lc, half=half, par=par: e.activation(
                    self.latT[:, lc * 1024 + half * 512:
                              lc * 1024 + (half + 1) * 512],
                    self.pb[2 * par + half][:], AF.Copy), inc="a_f1")
            self.bank_used(2 * par, "a_f1")
            self.bank_used(2 * par + 1, "a_f1")
        PEF1 = P.cnt("pe_f1")
        AF1FL = P.cnt("a_f1")

        ex_used = {0: 0, 1: 0}
        for mc in range(NLC):
            par = mc % 2
            if mc < 2:
                P.wait("g", "pe_f1", PEF1)
            else:
                P.wait("g", "pe_ex", ex_used[par])
            P.emit("g", lambda e, mc=mc, par=par: e.dma_start(
                self.wfb[par][:, 0:NLC, :],
                self.expw_d[:, mc * 128:(mc + 1) * 128]
                .rearrange("(c p) j -> p c j", p=128)), inc=f"dwe{par}", n=16)
            P.wait("t", f"dwe{par}", P.cnt(f"dwe{par}"))
            if mc == 0:
                P.wait("t", "a_f1", AF1FL)
            self.bank_wait(4, 5)
            for lc in range(NLC):
                for half in range(2):
                    P.emit("t", lambda e, lc=lc, half=half, par=par: e.matmul(
                        self.pb[4 + half][:], self.wfb[par][:, lc, :],
                        self.latT[:, lc * 1024 + half * 512:
                                  lc * 1024 + (half + 1) * 512],
                        start=(lc == 0), stop=(lc == NLC - 1)),
                        inc="pe_ex" if (lc == NLC - 1 and half == 1) else None)
            ex_used[par] = P.cnt("pe_ex")
            P.wait("v", "pe_ex", P.cnt("pe_ex"))
            if mc == 0:
                P.wait("v", "a_rwb", ARWB)
            for half in range(2):
                P.emit("v", lambda e, mc=mc, half=half: e.tensor_tensor(
                    self.moeT[:, mc * 1024 + half * 512:
                              mc * 1024 + (half + 1) * 512],
                    self.pb[4 + half][:],
                    self.rw_bc[:, half * 512:(half + 1) * 512], ALU.mult),
                    inc="v_moe")
            self.bank_used(4, "v_moe")
            self.bank_used(5, "v_moe")
        PEEX = P.cnt("pe_ex")
        VMOE = P.drain_inc("v", "v_moef")

        pre_base = {0: P.cnt("dpre0"), 1: P.cnt("dpre1")}
        up_used = {0: 0, 1: 0}
        re_used = {0: 0, 1: 0}
        pre_used = {0: 0, 1: 0}
        for q in range(4):
            dn_before = P.cnt("pe_dn")
            for il in range(16):
                par = il % 2
                if q == 0 and il < 2:
                    P.wait("g", "pe_ex", PEEX)
                else:
                    P.wait("g", "pe_up", up_used[par])
                P.emit("g", lambda e, q=q, il=il, par=par: e.dma_start(
                    self.wfb[par][:],
                    self.upw_d[:, (16 * q + il) * 128:(16 * q + il + 1) * 128]
                    .rearrange("(c p) j -> p c j", p=128)),
                    inc=f"dwu{par}", n=16)
                P.wait("t", f"dwu{par}", P.cnt(f"dwu{par}"))
                self.bank_wait(4, 5)
                for hc in range(NHC):
                    for half in range(2):
                        P.emit("t", lambda e, hc=hc, half=half, par=par:
                               e.matmul(self.pb[4 + half][:],
                                        self.wfb[par][:, hc, :],
                                        self.l2t[:, hc * 1024 + half * 512:
                                                 hc * 1024 + (half + 1) * 512],
                                        start=(hc == 0), stop=(hc == NHC - 1)),
                               inc="pe_up" if (hc == NHC - 1 and half == 1)
                               else None)
                up_used[par] = P.cnt("pe_up")
                P.wait("a", "pe_up", P.cnt("pe_up"))
                for half in range(2):
                    rpar = half
                    P.wait("a", "v_hsq", re_used[rpar])
                    P.emit("a", lambda e, half=half, rpar=rpar: e.activation(
                        self.relust[rpar][:], self.pb[4 + half][:], AF.Relu),
                        inc="a_re")
                    P.wait("v", "a_re", P.cnt("a_re"))
                    if il == 0 and half == 0:
                        if q == 0:
                            P.wait("v", "pe_ex", PEEX)
                        else:
                            P.wait("v", "pe_dn", dn_before)
                    P.emit("v", lambda e, il=il, half=half, rpar=rpar:
                           e.tensor_mul(
                               self.hTq[:, il * 1024 + half * 512:
                                        il * 1024 + (half + 1) * 512],
                               self.relust[rpar][:], self.relust[rpar][:]),
                           inc="v_hsq")
                    re_used[rpar] = P.cnt("v_hsq")
                self.bank_used(4, "a_re")
                self.bank_used(5, "a_re")
            VHSQ = P.drain_inc("v", "v_hsqf")

            dw_used = {0: 0, 1: 0}
            for ocs in range(8):
                dwv = {}
                for c8 in range(2):
                    P.wait("g", "pe_dn", dw_used[c8])
                    P.emit("g", lambda e, q=q, ocs=ocs, c8=c8: e.dma_start(
                        self.dwb[c8][:],
                        self.downw_d[(16 * q + 8 * c8) * 128:
                                     (16 * q + 8 * c8 + 8) * 128,
                                     ocs * 512:(ocs + 1) * 512]
                        .rearrange("(c p) j -> p c j", p=128)),
                        inc=f"ddw{c8}", n=16)
                    dwv[c8] = P.cnt(f"ddw{c8}")
                if q == 3:
                    if ocs == 0:
                        P.wait("g", "pe_up", P.cnt("pe_up"))
                    else:
                        P.wait("g", "pe_dn", dw_used[0])
                    P.emit("g", lambda e, ocs=ocs: e.dma_start(
                        self.fc2b[:],
                        self.fc2w_d[:, ocs * 512:(ocs + 1) * 512]
                        .rearrange("(c p) j -> p c j", p=128)),
                        inc="dfc2", n=16)
                if ocs == 0:
                    P.wait("t", "v_hsqf", VHSQ)
                self.bank_wait(*range(8))
                for il in range(16):
                    c8 = il // 8
                    if il % 8 == 0:
                        P.wait("t", f"ddw{c8}", dwv[c8])
                    for tb in range(8):
                        inc = ("pe_dn" if (q != 3 and il == 15 and tb == 7)
                               else None)
                        P.emit("t", lambda e, il=il, tb=tb, c8=c8, q=q:
                               e.matmul(
                                   self.pb[tb][:],
                                   self.hTq[:, il * 1024 + tb * 128:
                                            il * 1024 + (tb + 1) * 128],
                                   self.dwb[c8][:, il % 8, :],
                                   start=(il == 0),
                                   stop=(q != 3 and il == 15)), inc=inc)
                if q == 3:
                    P.wait("t", "dfc2", P.cnt("dfc2"))
                    if ocs == 0:
                        P.wait("t", "v_moef", VMOE)
                    for lc in range(NLC):
                        for tb in range(8):
                            P.emit("t", lambda e, lc=lc, tb=tb: e.matmul(
                                self.pb[tb][:],
                                self.moeT[:, lc * 1024 + tb * 128:
                                          lc * 1024 + (tb + 1) * 128],
                                self.fc2b[:, lc, :],
                                start=False, stop=(lc == NLC - 1)),
                                inc="pe_dn" if (lc == NLC - 1 and tb == 7)
                                else None)
                pednv = P.cnt("pe_dn")
                dw_used[0] = pednv
                dw_used[1] = pednv
                for tb in range(8):
                    par = tb % 2
                    P.wait("a", "pe_dn", pednv)
                    P.wait("a", f"dpre{par}", pre_used[par])
                    P.emit("a", lambda e, tb=tb, par=par: e.activation(
                        self.prest[par][:], self.pb[tb][:], AF.Copy),
                        inc="a_pre")
                    self.bank_used(tb, "a_pre")
                    P.wait("g", "a_pre", P.cnt("a_pre"))
                    if q > 0 and ocs == 0 and tb == 0:
                        P.wait("g", "dpre0", pre_base[0] + 16 * 32 * q)
                        P.wait("g", "dpre1", pre_base[1] + 16 * 32 * q)
                    op = {} if q == 0 else {"accum_op": ALU.add}
                    P.emit("g", lambda e, tb=tb, ocs=ocs, mb=mb, par=par,
                           op=op: e.dma_start(
                               self.pre_scr[(mb * 8 + tb) * 128:
                                            (mb * 8 + tb + 1) * 128,
                                            ocs * 512:(ocs + 1) * 512],
                               self.prest[par][:], **op), inc=f"dpre{par}",
                           n=16)
                    pre_used[par] = P.cnt(f"dpre{par}")
        self.DPRE = (P.cnt("dpre0"), P.cnt("dpre1"))

    def emit_top2(self, tb):
        P = self
        l = self.pb[7]
        s2 = self.st2
        P.emit("v", lambda e: e.tensor_copy(s2[:, 0:1], l[:, 0:1]))
        for i in range(1, 7):
            P.chain("v")
            P.emit("v", lambda e, i=i: e.tensor_max(
                s2[:, i:i + 1], s2[:, i - 1:i], l[:, i:i + 1]))
        P.emit("v", lambda e: e.tensor_copy(s2[:, 13:14], l[:, 7:8]))
        for j in range(6, 0, -1):
            P.chain("v")
            P.emit("v", lambda e, j=j: e.tensor_max(
                s2[:, 6 + j:7 + j], s2[:, 7 + j:8 + j], l[:, j:j + 1]))
        P.chain("v")
        P.emit("v", lambda e: e.tensor_add(s2[:, 16:17], l[:, 0:1], s2[:, 7:8]))
        P.emit("v", lambda e: e.tensor_add(s2[:, 23:24], l[:, 7:8], s2[:, 6:7]))
        for i in range(1, 7):
            P.emit("v", lambda e, i=i: e.tensor_max(
                s2[:, 24 + i:25 + i], s2[:, i - 1:i], s2[:, 7 + i:8 + i]))
        P.chain("v")
        for i in range(1, 7):
            P.emit("v", lambda e, i=i: e.tensor_add(
                s2[:, 16 + i:17 + i], l[:, i:i + 1], s2[:, 24 + i:25 + i]))
        P.chain("v")
        P.emit("v", lambda e: e.tensor_max(s2[:, 32:33], s2[:, 16:17], s2[:, 17:18]))
        for i in range(2, 8):
            P.chain("v")
            P.emit("v", lambda e, i=i: e.tensor_max(
                s2[:, 32:33], s2[:, 32:33], s2[:, 16 + i:17 + i]))
        P.chain("v")
        P.emit("v", lambda e, tb=tb: e.tensor_copy(
            self.rw_nat[:, tb:tb + 1], s2[:, 32:33]), inc="v_rw")

    def ph_D5(self, mb):
        P = self
        P.emit("s", lambda e: e.dma_start(self.gbc[:], self.gamma_d),
               inc="dgb", n=16)
        P.emit("s", lambda e: e.dma_start(self.bbc[:], self.beta_d),
               inc="dgb", n=16)
        dgb = P.cnt("dgb")
        used = {0: 0, 1: 0}
        dy_used = {0: 0, 1: 0}
        for tb in range(8):
            par = tb % 2
            P.wait("s", "dpre0", self.DPRE[0])
            P.wait("s", "dpre1", self.DPRE[1])
            P.wait("s", "dx2s0", P.cnt("dx2s0"))
            P.wait("s", "dx2s1", P.cnt("dx2s1"))
            if used[par]:
                P.wait("s", "v_y", used[par])
            P.emit("s", lambda e, tb=tb, mb=mb, par=par: e.dma_start(
                self.pret[par][:],
                self.pre_scr[(mb * 8 + tb) * 128:(mb * 8 + tb + 1) * 128, :]),
                inc=f"dpret{par}", n=16)
            P.emit("s", lambda e, tb=tb, mb=mb, par=par: e.dma_start(
                self.x2f[par][:],
                self.x2_scr[(mb * 8 + tb) * 128:(mb * 8 + tb + 1) * 128, :]),
                inc=f"dpret{par}", n=16)
            dv = P.cnt(f"dpret{par}")
            lnv = self.emit_ln_d5(par, ("dpret" + str(par), dv), dgb, tb,
                                  dy_used[par])
            used[par] = lnv
            P.wait("s", "v_y", lnv)
            P.emit("s", lambda e, tb=tb, mb=mb, par=par: e.dma_start(
                self.y_d[(mb * 8 + tb) * 128:(mb * 8 + tb + 1) * 128, :],
                self.yst[par][:]), inc=f"dy{par}", n=16)
            dy_used[par] = P.cnt(f"dy{par}")

    def emit_ln_d5(self, par, src_ready, dgb, tb, dyfree=0):
        P = self
        nm, v = src_ready
        src = self.pret[par]
        stt = self.stt
        P.wait("a", nm, v)
        P.emit("a", lambda e: e.activation(self.tst[:], src[:], AF.Square),
               inc="a_sq")
        P.wait("v", nm, v)
        P.emit("v", lambda e: e.reduce_sum(stt[:, 0:1], src[:], axis=AX.X))
        P.wait("v", "a_sq", P.cnt("a_sq"))
        P.emit("v", lambda e: e.reduce_sum(stt[:, 1:2], self.tst[:], axis=AX.X))
        P.chain("v")
        P.emit("v", lambda e: e.tensor_scalar(
            stt[:, 2:3], stt[:, 0:1], 1.0 / HID, None, ALU.mult))
        P.emit("v", lambda e: e.tensor_scalar(
            stt[:, 3:4], stt[:, 1:2], 1.0 / HID, None, ALU.mult))
        P.chain("v")
        P.emit("v", lambda e: e.tensor_tensor(
            stt[:, 4:5], stt[:, 2:3], stt[:, 2:3], ALU.mult))
        P.chain("v")
        P.emit("v", lambda e: e.tensor_sub(
            stt[:, 5:6], stt[:, 3:4], stt[:, 4:5]), inc="v_var")
        P.wait("a", "v_var", P.cnt("v_var"))
        P.emit("a", lambda e: e.activation(
            stt[:, 6:7], stt[:, 5:6], AF.Ln, bias=self.epsv[:]), inc="a_ln")
        P.wait("a", "a_ln", P.cnt("a_ln"))
        P.emit("a", lambda e: e.activation(
            stt[:, 7:8], stt[:, 6:7], AF.Exp, scale=-0.5), inc="a_rstd")
        P.wait("v", "a_rstd", P.cnt("a_rstd"))
        P.emit("v", lambda e: e.tensor_scalar(
            self.tst[:], src[:], stt[:, 2:3], stt[:, 7:8],
            ALU.subtract, ALU.mult))
        if tb == 0:
            P.wait("v", "dgb", dgb)
        P.chain("v")
        P.emit("v", lambda e: e.tensor_mul(self.tst[:], self.tst[:], self.gbc[:]))
        P.chain("v")
        P.emit("v", lambda e: e.tensor_add(self.tst[:], self.tst[:], self.bbc[:]))
        if dyfree:
            P.wait("v", f"dy{par}", dyfree)
        P.chain("v")
        P.emit("v", lambda e, par=par: e.tensor_add(
            self.yst[par][:], self.tst[:], self.x2f[par][:]), inc="v_y")
        return P.cnt("v_y")

    def build(self):
        self.declare()
        self.ph_init()
        self.ph_A()
        if PHASES_UPTO >= 1:
            self.barrier("A_B0")
            self.ph_B0()
        if PHASES_UPTO >= 2:
            self.barrier("B0_B1")
            self.ph_B1()
        if PHASES_UPTO >= 3:
            for mb in range(2):
                self.barrier(f"pre_C{mb}")
                self.ph_C(mb)
                if PHASES_UPTO >= 4:
                    self.barrier(f"C_D{mb}")
                    self.ph_D(mb)
                    self.barrier(f"D_D5{mb}")
                    self.ph_D5(mb)
        for nm in ("dy0", "dy1", "ddbg", "dlnst0", "dlnst1", "dqts", "dats",
                   "dln2", "dpre0", "dpre1", "dx2s0", "dx2s1"):
            v = self.cnt(nm)
            if v:
                self.wait("g", nm, v)
        blk = self.ctx.enter_context(self.nc.Block())
        secs = {"t": blk.tensor, "v": blk.vector, "a": blk.scalar,
                "s": blk.sync, "g": blk.gpsimd}
        for key in "tvasg":
            ops = self.ops[key]

            def body(e, ops=ops):
                for fn in ops:
                    fn(e)
            secs[key](body)
        return self.nc


_CACHE = {}


def _get_program(taps=()):
    key = (tuple(sorted(taps)), PHASES_UPTO)
    if key not in _CACHE:
        mk = MK(debug_taps=taps)
        nc = mk.build()
        _CACHE[key] = (nc, mk)
    return _CACHE[key]


def kernel(L0_attn_hidden_states, L0_attn_Wqkv, L0_attn_Wo, L0_ffn_gate_w,
           L0_ffn_up_w, L0_ffn_down_w, L0_ffn_fc1_w, L0_ffn_fc2_w,
           L0_ffn_expert_w, L0_ffn_ln_gamma, L0_ffn_ln_beta):
    x = np.ascontiguousarray(np.asarray(L0_attn_hidden_states, dtype=np.float32))
    w = {
        "wqkv": np.ascontiguousarray(np.asarray(L0_attn_Wqkv, np.float32)),
        "wo": np.ascontiguousarray(np.asarray(L0_attn_Wo, np.float32)),
        "gatew": np.ascontiguousarray(np.asarray(L0_ffn_gate_w, np.float32)),
        "upw": np.ascontiguousarray(np.asarray(L0_ffn_up_w, np.float32)),
        "downw": np.ascontiguousarray(np.asarray(L0_ffn_down_w, np.float32)),
        "fc1w": np.ascontiguousarray(np.asarray(L0_ffn_fc1_w, np.float32)),
        "fc2w": np.ascontiguousarray(np.asarray(L0_ffn_fc2_w, np.float32)),
        "expw": np.ascontiguousarray(np.asarray(L0_ffn_expert_w, np.float32)),
    }
    gamma = np.ascontiguousarray(np.broadcast_to(
        np.asarray(L0_ffn_ln_gamma, np.float32), (128, HID)))
    beta = np.ascontiguousarray(np.broadcast_to(
        np.asarray(L0_ffn_ln_beta, np.float32), (128, HID)))
    gtabs = [_g_table(0), _g_table(1)]
    mtabs = [_mask_table(0), _mask_table(1)]

    nc, _mk = _get_program(DEBUG_TAPS)
    in_maps = []
    for c in range(8):
        b, p = c // 2, c % 2
        xb = x[b]
        x_own = np.ascontiguousarray(np.concatenate(
            [xb[512 * gs:512 * (gs + 1)] for gs in STRIPS[p]], axis=0))
        im = {"x_all": np.ascontiguousarray(xb), "x_own": x_own,
              "gamma_bc": gamma, "beta_bc": beta,
              "gtab": gtabs[p], "masktab": mtabs[p]}
        im.update(w)
        in_maps.append(im)

    res = run_bass_kernel_spmd(nc, in_maps, core_ids=list(range(8)))
    kernel._last_results = res.results

    y = np.empty((B, S, HID), np.float32)
    for c in range(8):
        b, p = c // 2, c % 2
        yo = res.results[c]["y"]
        for ss, gs in enumerate(STRIPS[p]):
            y[b, 512 * gs:512 * (gs + 1)] = yo[512 * ss:512 * (ss + 1)]
    return y


# version 04 (final)
# speedup vs baseline: 1.2958x; 1.0609x over previous
import math
from contextlib import ExitStack

import numpy as np
import ml_dtypes

import concourse.bass as bass
import concourse.mybir as mybir
from concourse.bass_utils import run_bass_kernel_spmd

F32 = mybir.dt.float32
BF16 = mybir.dt.bfloat16
AF = mybir.ActivationFunctionType
ALU = mybir.AluOpType
AX = mybir.AxisListType

B, S, HID = 4, 4096, 4096
NH, DH = 32, 128
INT, LAT, NE = 8192, 2048, 8
NHC, NIC, NLC = 32, 64, 16
EPS = 1e-5
SCALE = 1.0 / math.sqrt(DH)
STRIPS = [[0, 3, 4, 7], [1, 2, 5, 6]]
PC = [8, 16, 24, 32]
SLOT_BASE = [0, 8, 24, 48]
TOT_SLOTS = 80

SBUF_BASE = 16512
SBUF_TOP = 229344

PHASES_UPTO = 99
DEBUG_TAPS = ()


def _alibi_slopes():
    n = 32
    cp2 = 2 ** int(math.floor(math.log2(n)))
    base = 2.0 ** (-(2.0 ** (-(math.log2(cp2) - 3))))
    return base ** np.arange(1, cp2 + 1, dtype=np.float64)


def _diag_positions(ss):
    out = []
    for g in (STRIPS[0][ss], STRIPS[1][ss]):
        for m in range(4):
            out.append((4 * g + m, m))
    return out


def _g_table(p):
    sl = _alibi_slopes()
    cols = np.zeros((TOT_SLOTS * 128,), np.float64)
    for ss in range(4):
        gs = STRIPS[p][ss]
        for j in range(PC[ss]):
            slot = SLOT_BASE[ss] + j
            for h in range(32):
                for qb in range(4):
                    d = j - (4 * gs + qb)
                    g = 0.0 if d > 0 else math.exp(sl[h] * 64.0 * d)
                    cols[(slot * 32 + h) * 4 + qb] = g
    tbl = np.broadcast_to(cols.astype(np.float32), (128, TOT_SLOTS * 128))
    return np.ascontiguousarray(tbl)


def _mask_table(p):
    tri = np.tril(np.ones((128, 128), np.float32)).T
    ones = np.ones((128, 128), np.float32)
    blocks = []
    for ss in range(4):
        gs = STRIPS[p][ss]
        for (j, m) in _diag_positions(ss):
            blocks.append(tri if j == 4 * gs + m else ones)
    return np.ascontiguousarray(
        np.concatenate(blocks, axis=1).astype(ml_dtypes.bfloat16))


def _alibi_kk():
    sl = _alibi_slopes()
    kk = np.arange(128, dtype=np.float64) - 64.0
    return np.ascontiguousarray(np.outer(kk, sl).astype(np.float32))


class MK:
    def __init__(self, debug_taps=()):
        self.nc = bass.Bass("TRN2", target_bir_lowering=False)
        self.ctx = ExitStack()
        self.taps = set(debug_taps)
        self.ops = {e: [] for e in "tvasg"}
        self.sems = {}
        self.last_bank = {}
        self.sboff = SBUF_BASE
        self.marks = {}

    def semh(self, name):
        if name not in self.sems:
            h = self.ctx.enter_context(self.nc.semaphore(name))
            self.sems[name] = [h, 0]
        return self.sems[name]

    def cnt(self, name):
        return self.semh(name)[1]

    def wait(self, eng, name, val):
        if val <= 0:
            return
        h = self.semh(name)[0]
        self.ops[eng].append(lambda e, h=h, v=val: e.wait_ge(h, v))

    def emit(self, eng, fn, inc=None, n=1):
        if inc is None:
            self.ops[eng].append(fn)
            return None
        s = self.semh(inc)
        s[1] += n
        if eng in "va":
            self.ops[eng].append(lambda e, fn=fn: fn(e))
            self.ops[eng].append(
                lambda e, h=s[0], n=n: e.drain(fusable=True).then_inc(h, n))
        else:
            self.ops[eng].append(
                lambda e, fn=fn, h=s[0], n=n: fn(e).then_inc(h, n))
        return s[1]

    def drain_inc(self, eng, name, n=1):
        s = self.semh(name)
        s[1] += n
        self.ops[eng].append(
            lambda e, h=s[0], n=n: e.drain(fusable=True).then_inc(h, n))
        return s[1]

    def barrier(self, tag):
        name = "bar_" + tag
        for e in "tvasg":
            self.emit(e, lambda en: en.nop(), inc=name)
        v = self.cnt(name)
        for e in "tvasg":
            self.wait(e, name, v)

    def bank_wait(self, *banks):
        for b in banks:
            if b in self.last_bank:
                nm, v = self.last_bank.pop(b)
                self.wait("t", nm, v)

    def bank_used(self, b, semname):
        self.last_bank[b] = (semname, self.cnt(semname))

    def mark(self, name):
        self.marks[name] = self.sboff

    def reset(self, name):
        self.sboff = self.marks[name]

    def al(self, name, shape, dtype, align=64, at=None):
        if at is None:
            per = int(np.prod(shape[1:])) * mybir.dt.size(dtype)
            self.sboff = (self.sboff + align - 1) // align * align
            at = self.sboff
            self.sboff += per
        assert self.sboff <= SBUF_TOP, f"SBUF overflow at {name}: {self.sboff}"
        return self.nc.alloc_sbuf_tensor_at(name, list(shape), dtype, offset=at)

    def din(self, name, shape, dtype=F32):
        return self.nc.dram_tensor(name, list(shape), dtype,
                                   kind="ExternalInput").ap()

    def dscr(self, name, shape, dtype):
        kind = "ExternalOutput" if name in self.taps else "Internal"
        return self.nc.dram_tensor(name, list(shape), dtype, kind=kind).ap()

    def declare(self):
        nc = self.nc
        self.x_all = self.din("x_all", [S, HID])
        self.x_own = self.din("x_own", [2048, HID])
        self.wqkv = self.din("wqkv", [HID, 4352])
        self.wo_d = self.din("wo", [HID, HID])
        self.gatew_d = self.din("gatew", [HID, NE])
        self.upw_d = self.din("upw", [HID, INT])
        self.downw_d = self.din("downw", [INT, HID])
        self.fc1w_d = self.din("fc1w", [HID, LAT])
        self.fc2w_d = self.din("fc2w", [LAT, HID])
        self.expw_d = self.din("expw", [LAT, LAT])
        self.gamma_d = self.din("gamma_bc", [128, HID])
        self.beta_d = self.din("beta_bc", [128, HID])
        self.gtab_d = self.din("gtab", [128, TOT_SLOTS * 128])
        self.mask_d = self.din("masktab", [128, 32 * 128], BF16)

        self.y_d = nc.dram_tensor("y", [2048, HID], F32,
                                  kind="ExternalOutput").ap()

        self.ln1_scr = self.dscr("ln1_scr", [S, HID], BF16)
        self.ln1o_scr = self.dscr("ln1o_scr", [2048, HID], BF16)
        self.qt_scr = self.dscr("qt_scr", [4, 128, NH * 512], BF16)
        self.at_scr = self.dscr("at_scr", [4, 128, NHC * 512], BF16)
        self.x2_scr = self.dscr("x2_scr", [2048, HID], F32)
        self.ln2_scr = self.dscr("ln2_scr", [2048, HID], BF16)
        self.pre_scr = self.dscr("pre_scr", [2048, HID], F32)
        self.rw_scr = self.dscr("rw_scr", [1, 1024], F32)
        self.den_scr = self.dscr("den_scr", [32, 512], F32)
        self.kt2_scr = self.dscr("kt2_scr", [128, S], F32)
        self.s0_scr = self.dscr("s0_scr", [128, 512], F32)
        self.pt_scr = self.dscr("pt_scr", [8, 128, 512], F32)
        self.pe_scr = self.dscr("pe_scr", [8, 128, 512], F32)
        self.kv_scr = self.dscr("kv_scr", [2, 128, S], F32)
        self.rwdbg_scr = self.dscr("rwdbg_scr", [128, 16], F32)

        self.onesb_d = nc.inline_tensor(
            np.ones((128, 1), ml_dtypes.bfloat16), "onesb_cst")
        self.onesf_d = nc.inline_tensor(np.ones((1, 128), np.float32), "onesf_cst")
        self.identf_d = nc.inline_tensor(np.eye(128, dtype=np.float32), "identf_cst")
        self.alibi_d = nc.inline_tensor(_alibi_kk(), "alibi_cst")
        self.epsv_d = nc.inline_tensor(np.full((128, 1), EPS, np.float32), "eps_cst")

        al = self.al
        self.KT = al("KT", [128, S], BF16)
        self.VN = al("VN", [128, S], BF16)
        self.onesb = al("onesb", [128, 1], BF16)
        self.onesf = al("onesf", [1, 128], F32)
        self.identf = al("identf", [128, 128], F32)
        self.alibi = al("alibi", [128, 32], F32)
        self.epsv = al("epsv", [128, 1], F32)
        self.gatew = al("gatew_sb", [128, NHC, 8], BF16)
        self.stt = al("stt", [128, 16], F32)
        self.st2 = al("st2", [128, 64], F32)
        self.rw_nat = al("rw_nat", [128, 8], F32)
        self.rwT = al("rwT", [8, 128], F32)
        self.rw_row = al("rw_row", [1, 1024], F32)
        self.recip = al("recip", [1, 1024], F32)
        self.rb_sb = [al(f"rb_sb{i}", [128, 512], F32) for i in range(2)]
        self.flushb = al("flushb", [1, 512], F32)
        self.denst = al("denst", [1, 512], F32)
        self.dmpst = al("dmpst", [128, 512], F32)
        self.rw_bc = al("rw_bc", [128, 1024], F32)
        self.mark("shared")

        self.xa = [al(f"xa{i}", [128, HID], F32) for i in range(2)]
        self.sqst = al("sqst", [128, HID], F32)
        self.lnst = [al(f"lnst{i}", [128, HID], BF16) for i in range(2)]
        self.l1t = al("l1t", [128, NHC * 512], BF16)
        self.wkv = al("wkv", [128, NHC, 256], BF16)

        self.reset("shared")
        self.l1ot = al("l1ot", [128, NHC * 512], BF16)
        self.wq = [al(f"wq{i}", [128, NHC, 128], BF16) for i in range(2)]
        self.qT0 = al("qT0", [128, NH * 512], BF16)

        self.reset("shared")
        self.qt = al("qt", [128, NH * 512], BF16)
        self.attnT = al("attnT", [128, NHC * 512], BF16)
        self.masks = al("masks", [128, 32, 128], BF16)
        self.biasb = al("biasb", [128, 4096], F32)
        self.peh = [al(f"peh{i}", [128, 512], BF16) for i in range(2)]
        self.ptb = [al(f"ptb{i}", [128, 512], BF16) for i in range(3)]

        self.reset("shared")
        self.amac = al("amac", [128, NHC, 2, 512], BF16)
        self.wob = [al(f"wob{i}", [128, 4, 512], BF16) for i in range(2)]
        self.xsl = [al(f"xsl{i}", [128, 512], F32) for i in range(2)]
        self.x2st = [al(f"x2st{i}", [128, 512], F32) for i in range(2)]
        self.x2t = al("x2t", [128, HID], F32)
        self.sqln = al("sqln", [128, HID], F32)
        self.ln2o = al("ln2o", [128, HID], BF16)

        self.reset("shared")
        self.l2t = al("l2t", [128, NHC * 1024], BF16)
        self.hTq = al("hTq", [128, 16 * 1024], BF16)
        self.latT = al("latT", [128, NLC * 1024], BF16,
                       at=self.nc.lookup_mloc(self.hTq).addr)
        self.moeT = al("moeT", [128, NLC * 1024], BF16)
        self.wfb = [al(f"wfb{i}", [128, NHC, 128], BF16) for i in range(2)]
        self.fc2b = al("fc2b", [128, NLC, 512], BF16,
                       at=self.nc.lookup_mloc(self.wfb[0]).addr)
        self.dwb = [al(f"dwb{i}", [128, 8, 512], BF16) for i in range(2)]
        self.relust = [al(f"relust{i}", [128, 512], BF16) for i in range(2)]
        self.prest = [al(f"prest{i}", [128, 512], F32) for i in range(2)]

        self.reset("shared")
        self.pret = [al(f"pret{i}", [128, HID], F32) for i in range(2)]
        self.x2f = [al(f"x2f{i}", [128, HID], F32) for i in range(2)]
        self.gbc = al("gbc", [128, HID], F32)
        self.bbc = al("bbc", [128, HID], F32)
        self.tst = al("tst", [128, HID], F32)
        self.yst = [al(f"yst{i}", [128, HID], F32) for i in range(2)]

        self.pb = [self.ctx.enter_context(
            self.nc.psum_tensor(f"pb{i}", [128, 512], F32)) for i in range(8)]

    def chain(self, eng, name=None):
        name = name or ("ch_" + eng)
        v = self.drain_inc(eng, name)
        self.wait(eng, name, v)

    def emit_ln(self, src, sqbuf, dst, src_ready, dst_free=None):
        P = self
        nm, v = src_ready
        stt = self.stt
        P.wait("a", nm, v)
        P.emit("a", lambda e, src=src, sqbuf=sqbuf: e.activation(
            sqbuf[:], src[:], AF.Square), inc="a_sq")
        asq = P.cnt("a_sq")
        P.wait("v", nm, v)
        P.emit("v", lambda e, src=src: e.reduce_sum(
            stt[:, 0:1], src[:], axis=AX.X))
        P.wait("v", "a_sq", asq)
        P.emit("v", lambda e, sqbuf=sqbuf: e.reduce_sum(
            stt[:, 1:2], sqbuf[:], axis=AX.X))
        P.chain("v")
        P.emit("v", lambda e: e.tensor_scalar(
            stt[:, 2:3], stt[:, 0:1], 1.0 / HID, None, ALU.mult))
        P.emit("v", lambda e: e.tensor_scalar(
            stt[:, 3:4], stt[:, 1:2], 1.0 / HID, None, ALU.mult))
        P.chain("v")
        P.emit("v", lambda e: e.tensor_tensor(
            stt[:, 4:5], stt[:, 2:3], stt[:, 2:3], ALU.mult))
        P.chain("v")
        P.emit("v", lambda e: e.tensor_sub(
            stt[:, 5:6], stt[:, 3:4], stt[:, 4:5]), inc="v_var")
        P.wait("a", "v_var", P.cnt("v_var"))
        P.emit("a", lambda e: e.activation(
            stt[:, 6:7], stt[:, 5:6], AF.Ln, bias=self.epsv[:]), inc="a_ln")
        P.wait("a", "a_ln", P.cnt("a_ln"))
        P.emit("a", lambda e: e.activation(
            stt[:, 7:8], stt[:, 6:7], AF.Exp, scale=-0.5), inc="a_rstd")
        ar = P.cnt("a_rstd")
        if dst_free is not None and dst_free[1] > 0:
            P.wait("v", dst_free[0], dst_free[1])
        P.wait("v", "a_rstd", ar)
        P.emit("v", lambda e, src=src, dst=dst: e.tensor_scalar(
            dst[:], src[:], stt[:, 2:3], stt[:, 7:8],
            ALU.subtract, ALU.mult), inc="v_ln")
        return P.cnt("v_ln")

    def ph_init(self):
        P = self
        for buf, dr in ((self.onesb, self.onesb_d), (self.onesf, self.onesf_d),
                        (self.identf, self.identf_d), (self.alibi, self.alibi_d),
                        (self.epsv, self.epsv_d)):
            P.emit("s", lambda e, b=buf, d=dr: e.dma_start(b[:], d.ap()),
                   inc="dcst", n=16)
        self.CST = P.cnt("dcst")
        P.emit("g", lambda e: e.dma_start(
            self.wkv[:], self.wqkv[:, 4096:4352]
            .rearrange("(c p) j -> p c j", p=128)), inc="dwkv", n=16)
        for hc in range(NHC):
            P.emit("g", lambda e, hc=hc: e.dma_start(
                self.gatew[:, hc, :],
                self.gatew_d[hc * 128:(hc + 1) * 128, :]), inc="dgate", n=16)
        self.GATEW = P.cnt("dgate")

    def ph_A(self):
        P = self
        for e in "tvasg":
            P.wait(e, "dcst", self.CST)
            P.wait(e, "dwkv", P.cnt("dwkv"))
            P.wait(e, "dgate", self.GATEW)
        lnused = {0: 0, 1: 0}
        lnst_stored = {0: 0, 1: 0}
        kvdone = 0

        def ln_tile(src_rows_ap, dst_rows_ap, par):
            P.wait("s", "v_ln_used", lnused[par])
            P.emit("s", lambda e: e.dma_start(self.xa[par][:], src_rows_ap),
                   inc=f"dxa{par}", n=16)
            dv = P.cnt(f"dxa{par}")
            lnv = self.emit_ln(self.xa[par], self.sqst, self.lnst[par],
                               (f"dxa{par}", dv),
                               dst_free=(f"dlnst{par}", lnst_stored[par]))
            lnused[par] = P.drain_inc("v", "v_ln_used")
            P.wait("s", "v_ln", lnv)
            P.emit("s", lambda e: e.dma_start(dst_rows_ap, self.lnst[par][:]),
                   inc=f"dlnst{par}", n=16)
            lnst_stored[par] = P.cnt(f"dlnst{par}")

        for gs in range(8):
            for t4 in range(4):
                gt = gs * 4 + t4
                ln_tile(self.x_all[gt * 128:(gt + 1) * 128, :],
                        self.ln1_scr[gt * 128:(gt + 1) * 128, :], gt % 2)
                if gt == 0 and "rwdbg_scr" in self.taps:
                    P.wait("s", "v_ln", P.cnt("v_ln"))
                    P.emit("s", lambda e: e.dma_start(
                        self.rwdbg_scr[:], self.stt[:]), inc="ddbg", n=16)
                    P.wait("s", "ddbg", P.cnt("ddbg"))
                    P.wait("v", "ddbg", P.cnt("ddbg"))
                    P.wait("a", "ddbg", P.cnt("ddbg"))
            P.wait("s", f"dlnst0", lnst_stored[0])
            P.wait("s", f"dlnst1", lnst_stored[1])
            if gs > 0:
                P.wait("s", "pe_v", kvdone)
            for hc in range(NHC):
                P.emit("s", lambda e, gs=gs, hc=hc: e.dma_start(
                    self.l1t[:, hc * 512:(hc + 1) * 512],
                    self.ln1_scr[gs * 512:(gs + 1) * 512,
                                 hc * 128:(hc + 1) * 128],
                    transpose=True), inc="dl1t", n=16)
            dl1t = P.cnt("dl1t")
            P.wait("t", "dwkv", 16)
            P.wait("t", "dl1t", dl1t)
            self.bank_wait(0)
            for hc in range(NHC):
                P.emit("t", lambda e, hc=hc: e.matmul(
                    self.pb[0][:], self.wkv[:, hc, 0:128],
                    self.l1t[:, hc * 512:(hc + 1) * 512],
                    start=(hc == 0), stop=(hc == NHC - 1)),
                    inc="pe_k" if hc == NHC - 1 else None)
            P.wait("a", "pe_k", P.cnt("pe_k"))
            P.emit("a", lambda e, gs=gs: e.activation(
                self.KT[:, gs * 512:(gs + 1) * 512], self.pb[0][:],
                AF.Copy, scale=SCALE), inc="a_kev")
            self.bank_used(0, "a_kev")
            for t4 in range(4):
                self.bank_wait(1)
                for hc in range(NHC):
                    P.emit("t", lambda e, hc=hc, t4=t4: e.matmul(
                        self.pb[1][:, 0:128],
                        self.l1t[:, hc * 512 + t4 * 128:
                                 hc * 512 + (t4 + 1) * 128],
                        self.wkv[:, hc, 128:256],
                        start=(hc == 0), stop=(hc == NHC - 1)),
                        inc="pe_v" if hc == NHC - 1 else None)
                P.wait("v", "pe_v", P.cnt("pe_v"))
                P.emit("v", lambda e, gs=gs, t4=t4: e.tensor_copy(
                    self.VN[:, (gs * 4 + t4) * 128:(gs * 4 + t4 + 1) * 128],
                    self.pb[1][:, 0:128]), inc="v_vev")
                self.bank_used(1, "v_vev")
            kvdone = P.cnt("pe_v")
            KVSEM = "pe_v"

        for ot in range(16):
            ln_tile(self.x_own[ot * 128:(ot + 1) * 128, :],
                    self.ln1o_scr[ot * 128:(ot + 1) * 128, :], ot % 2)
        self.DLNO = P.cnt("dlnst0") + P.cnt("dlnst1")

        self.LNST0 = P.cnt("dlnst0")
        self.LNST1 = P.cnt("dlnst1")

        if "kv_scr" in self.taps:
            P.wait("v", "a_kev", P.cnt("a_kev"))
            P.emit("v", lambda e: e.tensor_copy(self.sqst[:, 0:4096],
                                                self.KT[:]), inc="v_dbg")
            P.wait("s", "v_dbg", P.cnt("v_dbg"))
            P.emit("s", lambda e: e.dma_start(self.kv_scr[0], self.sqst[:, 0:4096]),
                   inc="ddbg", n=16)
            P.wait("v", "ddbg", P.cnt("ddbg"))
            P.emit("v", lambda e: e.tensor_copy(self.sqst[:, 0:4096],
                                                self.KT[:]), inc="v_dbg")
            P.wait("s", "v_dbg", P.cnt("v_dbg"))
            P.emit("s", lambda e: e.dma_start(self.kv_scr[1], self.sqst[:, 0:4096]),
                   inc="ddbg", n=16)
            P.wait("s", "ddbg", P.cnt("ddbg"))

    def ph_B0(self):
        P = self
        qbdone = 0
        qmm = {0: 0, 1: 0}
        for ss in range(4):
            if ss > 0:
                P.wait("s", "pe_qmm", qbdone)
                P.wait("a", "dqts", P.cnt("dqts"))
            P.wait("s", "dlnst0", self.LNST0)
            P.wait("s", "dlnst1", self.LNST1)
            for hc in range(NHC):
                P.emit("s", lambda e, ss=ss, hc=hc: e.dma_start(
                    self.l1ot[:, hc * 512:(hc + 1) * 512],
                    self.ln1o_scr[ss * 512:(ss + 1) * 512,
                                  hc * 128:(hc + 1) * 128],
                    transpose=True), inc="dl1ot", n=16)
            dl1ot = P.cnt("dl1ot")
            for oc in range(NH):
                par = oc % 2
                if qmm[par] == 0:
                    P.wait("g", "dlnst0", self.LNST0)
                    P.wait("g", "dlnst1", self.LNST1)
                P.wait("g", "pe_qmm", qmm[par])
                P.emit("g", lambda e, oc=oc, par=par: e.dma_start(
                    self.wq[par][:],
                    self.wqkv[:, oc * 128:(oc + 1) * 128]
                    .rearrange("(c p) j -> p c j", p=128)),
                    inc=f"dwq{par}", n=16)
                P.wait("t", f"dwq{par}", P.cnt(f"dwq{par}"))
                P.wait("t", "dl1ot", dl1ot)
                self.bank_wait(par)
                for hc in range(NHC):
                    P.emit("t", lambda e, hc=hc, par=par: e.matmul(
                        self.pb[par][:], self.wq[par][:, hc, :],
                        self.l1ot[:, hc * 512:(hc + 1) * 512],
                        start=(hc == 0), stop=(hc == NHC - 1)),
                        inc="pe_qmm" if hc == NHC - 1 else None)
                qmm[par] = P.cnt("pe_qmm")
                P.wait("a", "pe_qmm", qmm[par])
                P.emit("a", lambda e, oc=oc, par=par: e.activation(
                    self.qT0[:, oc * 512:(oc + 1) * 512], self.pb[par][:],
                    AF.Copy), inc="a_qev")
                self.bank_used(par, "a_qev")
            P.wait("a", "a_qev", P.cnt("a_qev"))
            P.drain_inc("a", "a_qfl")
            P.wait("s", "a_qfl", P.cnt("a_qfl"))
            P.emit("s", lambda e, ss=ss: e.dma_start(
                self.qt_scr[ss], self.qT0[:]), inc="dqts", n=16)
            qbdone = P.cnt("pe_qmm")
        self.DQTS = P.cnt("dqts")

    def ph_B1(self):
        P = self
        P.wait("t", "dcst", self.CST)
        P.wait("s", "dqts", self.DQTS)
        P.emit("s", lambda e: e.dma_start(self.masks[:].rearrange(
            "p a b -> p (a b)"), self.mask_d), inc="dmask", n=16)
        P.wait("v", "dmask", P.cnt("dmask"))
        if "kt2_scr" in self.taps:
            for ch in range(8):
                P.wait("v", "dkt2", P.cnt("dkt2"))
                P.emit("v", lambda e, ch=ch: e.tensor_copy(
                    self.dmpst[:], self.KT[:, ch * 512:(ch + 1) * 512]))
                P.emit("v", lambda e: e.tensor_copy(
                    self.stt[:, 13:14], self.dmpst[:, 511:512]), inc="v_kt2")
                P.wait("s", "v_kt2", P.cnt("v_kt2"))
                P.emit("s", lambda e, ch=ch: e.dma_start(
                    self.kt2_scr[:, ch * 512:(ch + 1) * 512], self.dmpst[:]),
                    inc="dkt2", n=16)
        exp_at_sbank = {2: 0, 3: 0, 0: 0}
        vp_of = {}
        gp_of = {}
        gpeh_free = {0: 0, 1: 0}
        pt_free = {0: 0, 1: 0, 2: 0}
        peh_free = {0: 0, 1: 0}
        avn_free = {0: 0, 1: 0}
        den_free = {0: 0, 1: 0}
        rb_free = {0: 0, 1: 0}
        rbmm_done = {0: 0, 1: 0}
        smm_all = 0
        dens = {}

        def emit_tail(h):
            hp = h % 2
            ss0 = self._cur_ss == 0
            avb = 4 if hp == 0 else 6
            deb = 5 if hp == 0 else 7
            rbb = 1
            if ss0 and "den_scr" in self.taps:
                P.wait("a", "pe_ptc", dens[h])
                deb_ = 5 if hp == 0 else 7
                P.emit("a", lambda e, deb_=deb_: e.activation(
                    self.denst[:], self.pb[deb_][0:1, :], AF.Copy))
                P.emit("a", lambda e: e.activation(
                    self.stt[0:1, 12:13], self.denst[0:1, 511:512], AF.Copy),
                    inc="a_dden")
                P.wait("s", "a_dden", P.cnt("a_dden"))
                P.emit("s", lambda e, h=h: e.dma_start(
                    self.den_scr[h:h + 1, :], self.denst[:]), inc="ddump2",
                    n=16)
                P.wait("v", "ddump2", P.cnt("ddump2"))
            P.wait("v", "pe_ptc", dens[h])
            if rb_free[hp]:
                P.wait("v", "pe_rb", rbmm_done[hp])
            P.emit("v", lambda e, deb=deb, hp=hp: e.reciprocal(
                self.recip[0:1, hp * 512:(hp + 1) * 512],
                self.pb[deb][0:1, :]), inc="v_recip")
            den_free[hp] = P.cnt("v_recip")
            self.bank_used(deb, "v_recip")
            P.wait("t", "v_recip", P.cnt("v_recip"))
            self.bank_wait(rbb)
            P.emit("t", lambda e, rbb=rbb, hp=hp: e.matmul(
                self.pb[rbb][:], self.onesf[:],
                self.recip[0:1, hp * 512:(hp + 1) * 512],
                start=True, stop=True), inc="pe_rb")
            rbmm_done[hp] = P.cnt("pe_rb")
            P.wait("a", "pe_rb", P.cnt("pe_rb"))
            P.emit("a", lambda e, rbb=rbb, hp=hp: e.activation(
                self.rb_sb[hp][:], self.pb[rbb][:], AF.Copy), inc="a_rbev")
            rb_free[hp] = P.cnt("a_rbev")
            self.bank_used(rbb, "a_rbev")
            P.wait("v", "a_rbev", rb_free[hp])
            P.emit("v", lambda e, avb=avb, h=h, hp=hp: e.tensor_tensor(
                self.attnT[:, h * 512:(h + 1) * 512], self.pb[avb][:],
                self.rb_sb[hp][:], ALU.mult), inc="v_norm")
            avn_free[hp] = P.cnt("v_norm")
            self.bank_used(avb, "v_norm")

        for ss in range(4):
            self._cur_ss = ss
            npc = PC[ss]
            dpos = _diag_positions(ss)
            if ss > 0:
                P.wait("s", "pe_ptc", smm_all)
                P.wait("s", "dats", P.cnt("dats"))
            P.wait("s", "dqts", self.DQTS)
            P.emit("s", lambda e, ss=ss: e.dma_start(
                self.qt[:], self.qt_scr[ss]), inc="dqt", n=16)
            dqt = P.cnt("dqt")
            c0 = SLOT_BASE[ss] * 128
            c1 = (SLOT_BASE[ss] + npc) * 128
            P.emit("s", lambda e, c0=c0, c1=c1: e.dma_start(
                self.biasb[:, 0:c1 - c0], self.gtab_d[:, c0:c1]),
                inc="dbias", n=16)
            dbias = P.cnt("dbias")
            P.wait("v", "dats", P.cnt("dats"))
            P.wait("v", "dbias", dbias)
            for h in range(NH):
                hp = h % 2
                avb = 4 if hp == 0 else 6
                deb = 5 if hp == 0 else 7
                if avn_free[hp]:
                    P.wait("t", "v_norm", avn_free[hp])
                if den_free[hp]:
                    P.wait("t", "v_recip", den_free[hp])
                self.last_bank.pop(avb, None)
                self.last_bank.pop(deb, None)
                pends = []

                def emit_denav(j, pti):
                    P.wait("t", "v_p", vp_of[(h, j)])
                    P.wait("t", "g_p", gp_of[(h, j)])
                    P.emit("t", lambda e, deb=deb, pti=pti, j=j, npc=npc:
                           e.matmul(self.pb[deb][0:1, :], self.onesb[:],
                                    self.ptb[pti][:],
                                    start=(j == 0), stop=(j == npc - 1)))
                    P.emit("t", lambda e, avb=avb, pti=pti, j=j, npc=npc:
                           e.matmul(self.pb[avb][:],
                                    self.VN[:, j * 128:(j + 1) * 128],
                                    self.ptb[pti][:],
                                    start=(j == 0), stop=(j == npc - 1)),
                           inc="pe_ptc")
                    pt_free[pti] = P.cnt("pe_ptc")

                for j in range(npc):
                    sbk = (2, 3, 0)[j % 3]
                    pti = j % 3
                    if j < 2 and h == 0:
                        P.wait("t", "dqt", dqt)
                    if exp_at_sbank[sbk]:
                        P.wait("t", "a_exp", exp_at_sbank[sbk])
                    P.emit("t", lambda e, j=j, h=h, sbk=sbk: e.matmul(
                        self.pb[sbk][:], self.KT[:, j * 128:(j + 1) * 128],
                        self.qt[:, h * 512:(h + 1) * 512],
                        start=True, stop=True), inc="pe_s")
                    P.wait("a", "pe_s", P.cnt("pe_s"))
                    if ss == 0 and j == 0 and h == 0:
                        P.wait("a", "dcst", self.CST)
                    pp = j % 2
                    if peh_free[pp]:
                        P.wait("a", "v_p", peh_free[pp])
                        P.wait("a", "g_p", gpeh_free[pp])
                    P.emit("a", lambda e, sbk=sbk, h=h, pp=pp: e.activation(
                        self.peh[pp][:], self.pb[sbk][:], AF.Exp,
                        bias=self.alibi[:, h:h + 1]), inc="a_exp")
                    exp_at_sbank[sbk] = P.cnt("a_exp")
                    P.wait("v", "a_exp", exp_at_sbank[sbk])
                    P.wait("g", "a_exp", exp_at_sbank[sbk])
                    if ss == 0 and j == 0 and h == 0:
                        P.wait("v", "dcst", self.CST)
                        P.wait("g", "dbias", dbias)
                        P.wait("g", "dmask", P.cnt("dmask"))
                    if pt_free[pti]:
                        P.wait("v", "pe_ptc", pt_free[pti])
                        P.wait("g", "pe_ptc", pt_free[pti])
                    for qb in range(4):
                        eng = "v" if qb < 2 else "g"
                        gc = (j * 32 + h) * 4 + qb
                        P.emit(eng, lambda e, pti=pti, pp=pp, qb=qb, gc=gc:
                               e.tensor_scalar(
                                   self.ptb[pti][:, qb * 128:(qb + 1) * 128],
                                   self.peh[pp][:, qb * 128:(qb + 1) * 128],
                                   self.biasb[:, gc:gc + 1],
                                   self.biasb[:, gc:gc + 1],
                                   ALU.mult, ALU.mult))
                    for i, (dj, dm) in enumerate(dpos):
                        if dj == j:
                            eng = "v" if dm < 2 else "g"
                            P.chain(eng)
                            P.emit(eng, lambda e, pti=pti, dm=dm, ss=ss, i=i:
                                   e.tensor_tensor(
                                       self.ptb[pti][:, dm * 128:(dm + 1) * 128],
                                       self.ptb[pti][:, dm * 128:(dm + 1) * 128],
                                       self.masks[:, ss * 8 + i, :], ALU.mult))
                    P.drain_inc("v", "v_p")
                    gp = P.drain_inc("g", "g_p")
                    peh_free[pp] = P.cnt("v_p")
                    gpeh_free[pp] = gp
                    vp_of[(h, j)] = P.cnt("v_p")
                    gp_of[(h, j)] = gp
                    pends.append((j, pti))
                    if len(pends) > 2:
                        emit_denav(*pends.pop(0))
                for pj, ppti in pends:
                    emit_denav(pj, ppti)
                dens[h] = P.cnt("pe_ptc")
                if h > 0:
                    emit_tail(h - 1)
            emit_tail(NH - 1)
            smm_all = P.cnt("pe_ptc")
            P.wait("v", "v_norm", P.cnt("v_norm"))
            P.drain_inc("v", "v_atfl")
            P.wait("s", "v_atfl", P.cnt("v_atfl"))
            P.emit("s", lambda e, ss=ss: e.dma_start(
                self.at_scr[ss], self.attnT[:]), inc="dats", n=16)
        self.DATS = P.cnt("dats")

    def ph_C(self, mb):
        P = self
        P.wait("s", "dats", self.DATS)
        for k in range(2):
            ss = 2 * mb + k
            P.emit("s", lambda e, ss=ss, k=k: e.dma_start(
                self.amac[:, :, k, :],
                self.at_scr[ss].rearrange("p (c j) -> p c j", j=512)),
                inc="damac", n=16)
        damac = P.cnt("damac")
        P.wait("t", "damac", damac)
        if not hasattr(self, "_wo_gchunk"):
            self._wo_gchunk = 0
        xsl_used = {0: 0, 1: 0}
        x2st_stored = {0: 0, 1: 0}
        for ocs in range(8):
            dwo_val = []
            for c4 in range(8):
                par = c4 % 2
                g = self._wo_gchunk
                self._wo_gchunk += 1
                P.wait("g", "pe_woc", g - 1)
                P.emit("g", lambda e, c4=c4, ocs=ocs, par=par: e.dma_start(
                    self.wob[par][:],
                    self.wo_d[c4 * 512:(c4 + 1) * 512,
                              ocs * 512:(ocs + 1) * 512]
                    .rearrange("(c p) j -> p c j", p=128)),
                    inc=f"dwo{par}", n=16)
                dwo_val.append(P.cnt(f"dwo{par}"))
            self.bank_wait(*range(8))
            for hc in range(NHC):
                c4 = hc // 4
                par = c4 % 2
                if hc % 4 == 0:
                    P.wait("t", f"dwo{par}", dwo_val[c4])
                for tb in range(8):
                    inc = None
                    if tb == 7 and hc % 4 == 3:
                        inc = "pe_woc"
                    P.emit("t", lambda e, hc=hc, tb=tb, par=par: e.matmul(
                        self.pb[tb][:], self.amac[:, hc, tb // 4,
                                                  (tb % 4) * 128:
                                                  (tb % 4 + 1) * 128],
                        self.wob[par][:, hc % 4, :],
                        start=(hc == 0), stop=(hc == NHC - 1)),
                        inc=inc)
            pewo_woc = P.cnt("pe_woc")
            for tb in range(8):
                par = tb % 2
                P.wait("s", "v_x2", xsl_used[par])
                P.emit("s", lambda e, tb=tb, ocs=ocs, mb=mb, par=par:
                       e.dma_start(self.xsl[par][:],
                                   self.x_own[(mb * 8 + tb) * 128:
                                              (mb * 8 + tb + 1) * 128,
                                              ocs * 512:(ocs + 1) * 512]),
                       inc=f"dxsl{par}", n=16)
                P.wait("v", f"dxsl{par}", P.cnt(f"dxsl{par}"))
                P.wait("v", "pe_woc", pewo_woc)
                P.wait("v", f"dx2s{par}", x2st_stored[par])
                P.emit("v", lambda e, tb=tb, par=par: e.tensor_add(
                    self.x2st[par][:], self.pb[tb][:], self.xsl[par][:]),
                    inc="v_x2")
                xsl_used[par] = P.cnt("v_x2")
                self.bank_used(tb, "v_x2")
                P.wait("s", "v_x2", P.cnt("v_x2"))
                P.emit("s", lambda e, tb=tb, ocs=ocs, mb=mb, par=par:
                       e.dma_start(self.x2_scr[(mb * 8 + tb) * 128:
                                               (mb * 8 + tb + 1) * 128,
                                               ocs * 512:(ocs + 1) * 512],
                                   self.x2st[par][:]), inc=f"dx2s{par}", n=16)
                x2st_stored[par] = P.cnt(f"dx2s{par}")
        P.wait("v", "dy0", P.cnt("dy0"))
        P.wait("v", "dy1", P.cnt("dy1"))
        for tb in range(8):
            P.wait("s", "dx2s0", P.cnt("dx2s0"))
            P.wait("s", "dx2s1", P.cnt("dx2s1"))
            if tb > 0:
                P.wait("s", "v_ln_used", P.cnt("v_ln_used"))
            P.emit("s", lambda e, tb=tb, mb=mb: e.dma_start(
                self.x2t[:], self.x2_scr[(mb * 8 + tb) * 128:
                                         (mb * 8 + tb + 1) * 128, :]),
                inc="dx2t", n=16)
            lnv = self.emit_ln(self.x2t, self.sqln, self.ln2o,
                               ("dx2t", P.cnt("dx2t")),
                               dst_free=("dln2", P.cnt("dln2")))
            P.drain_inc("v", "v_ln_used")
            P.wait("s", "v_ln", lnv)
            P.emit("s", lambda e, tb=tb, mb=mb: e.dma_start(
                self.ln2_scr[(mb * 8 + tb) * 128:(mb * 8 + tb + 1) * 128, :],
                self.ln2o[:]), inc="dln2", n=16)
        self.DLN2 = P.cnt("dln2")

    def ph_D(self, mb):
        P = self
        P.wait("s", "dln2", self.DLN2)
        for hc in range(NHC):
            P.emit("s", lambda e, hc=hc, mb=mb: e.dma_start(
                self.l2t[:, hc * 1024:(hc + 1) * 1024],
                self.ln2_scr[mb * 1024:(mb + 1) * 1024,
                             hc * 128:(hc + 1) * 128],
                transpose=True), inc="dl2t", n=16)
        dl2t = P.cnt("dl2t")

        P.wait("t", "dl2t", dl2t)
        P.wait("t", "dgate", self.GATEW)
        for tb in range(8):
            self.bank_wait(7)
            for hc in range(NHC):
                P.emit("t", lambda e, hc=hc, tb=tb: e.matmul(
                    self.pb[7][:, 0:8],
                    self.l2t[:, hc * 1024 + tb * 128:hc * 1024 + (tb + 1) * 128],
                    self.gatew[:, hc, :],
                    start=(hc == 0), stop=(hc == NHC - 1)),
                    inc="pe_g" if hc == NHC - 1 else None)
            P.wait("v", "pe_g", P.cnt("pe_g"))
            self.emit_top2(tb)
            self.bank_used(7, "v_rw")
        P.wait("t", "v_rw", P.cnt("v_rw"))
        self.bank_wait(6)
        P.emit("t", lambda e: e.transpose(
            self.pb[6][0:8, 0:128], self.rw_nat[:], self.identf[:]),
            inc="pe_rwt")
        P.wait("a", "pe_rwt", P.cnt("pe_rwt"))
        P.emit("a", lambda e: e.activation(
            self.rwT[:], self.pb[6][0:8, 0:128], AF.Copy), inc="a_rwt")
        self.bank_used(6, "a_rwt")
        P.wait("s", "a_rwt", P.cnt("a_rwt"))
        P.emit("s", lambda e: e.dma_start(
            self.rw_scr.rearrange("o (a b) -> (o a) b", b=128), self.rwT[:]),
            inc="drws", n=16)
        P.wait("s", "drws", P.cnt("drws"))
        P.emit("s", lambda e: e.dma_start(self.rw_row[:], self.rw_scr),
               inc="drwl", n=16)
        P.wait("t", "drwl", P.cnt("drwl"))
        for half in range(2):
            self.bank_wait(6)
            P.emit("t", lambda e, half=half: e.matmul(
                self.pb[6][:], self.onesf[:],
                self.rw_row[0:1, half * 512:(half + 1) * 512],
                start=True, stop=True), inc="pe_rwb")
            P.wait("a", "pe_rwb", P.cnt("pe_rwb"))
            P.emit("a", lambda e, half=half: e.activation(
                self.rw_bc[:, half * 512:(half + 1) * 512],
                self.pb[6][:], AF.Copy), inc="a_rwb")
            self.bank_used(6, "a_rwb")
        ARWB = P.cnt("a_rwb")
        if "rwdbg_scr" in self.taps and mb == 0:
            P.wait("s", "a_rwb", ARWB)
            P.emit("s", lambda e: e.dma_start(
                self.rwdbg_scr[:, 0:8], self.rw_nat[:]), inc="ddbg", n=16)

        f1_used = {0: 0, 1: 0}
        for lc in range(NLC):
            par = lc % 2
            P.wait("g", "pe_f1", f1_used[par])
            P.emit("g", lambda e, lc=lc, par=par: e.dma_start(
                self.wfb[par][:],
                self.fc1w_d[:, lc * 128:(lc + 1) * 128]
                .rearrange("(c p) j -> p c j", p=128)), inc=f"dwf{par}", n=16)
            P.wait("t", f"dwf{par}", P.cnt(f"dwf{par}"))
            self.bank_wait(2 * par, 2 * par + 1)
            for hc in range(NHC):
                for half in range(2):
                    P.emit("t", lambda e, hc=hc, half=half, par=par: e.matmul(
                        self.pb[2 * par + half][:], self.wfb[par][:, hc, :],
                        self.l2t[:, hc * 1024 + half * 512:
                                 hc * 1024 + (half + 1) * 512],
                        start=(hc == 0), stop=(hc == NHC - 1)),
                        inc="pe_f1" if (hc == NHC - 1 and half == 1) else None)
            f1_used[par] = P.cnt("pe_f1")
            P.wait("a", "pe_f1", P.cnt("pe_f1"))
            for half in range(2):
                P.emit("a", lambda e, lc=lc, half=half, par=par: e.activation(
                    self.latT[:, lc * 1024 + half * 512:
                              lc * 1024 + (half + 1) * 512],
                    self.pb[2 * par + half][:], AF.Copy), inc="a_f1")
            self.bank_used(2 * par, "a_f1")
            self.bank_used(2 * par + 1, "a_f1")
        PEF1 = P.cnt("pe_f1")
        AF1FL = P.cnt("a_f1")

        ex_used = {0: 0, 1: 0}
        for mc in range(NLC):
            par = mc % 2
            if mc < 2:
                P.wait("g", "pe_f1", PEF1)
            else:
                P.wait("g", "pe_ex", ex_used[par])
            P.emit("g", lambda e, mc=mc, par=par: e.dma_start(
                self.wfb[par][:, 0:NLC, :],
                self.expw_d[:, mc * 128:(mc + 1) * 128]
                .rearrange("(c p) j -> p c j", p=128)), inc=f"dwe{par}", n=16)
            P.wait("t", f"dwe{par}", P.cnt(f"dwe{par}"))
            if mc == 0:
                P.wait("t", "a_f1", AF1FL)
            self.bank_wait(4, 5)
            for lc in range(NLC):
                for half in range(2):
                    P.emit("t", lambda e, lc=lc, half=half, par=par: e.matmul(
                        self.pb[4 + half][:], self.wfb[par][:, lc, :],
                        self.latT[:, lc * 1024 + half * 512:
                                  lc * 1024 + (half + 1) * 512],
                        start=(lc == 0), stop=(lc == NLC - 1)),
                        inc="pe_ex" if (lc == NLC - 1 and half == 1) else None)
            ex_used[par] = P.cnt("pe_ex")
            P.wait("v", "pe_ex", P.cnt("pe_ex"))
            if mc == 0:
                P.wait("v", "a_rwb", ARWB)
            for half in range(2):
                P.emit("v", lambda e, mc=mc, half=half: e.tensor_tensor(
                    self.moeT[:, mc * 1024 + half * 512:
                              mc * 1024 + (half + 1) * 512],
                    self.pb[4 + half][:],
                    self.rw_bc[:, half * 512:(half + 1) * 512], ALU.mult),
                    inc="v_moe")
            self.bank_used(4, "v_moe")
            self.bank_used(5, "v_moe")
        PEEX = P.cnt("pe_ex")
        VMOE = P.drain_inc("v", "v_moef")

        pre_base = {0: P.cnt("dpre0"), 1: P.cnt("dpre1")}
        up_used = {0: 0, 1: 0}
        re_used = {0: 0, 1: 0}
        pre_used = {0: 0, 1: 0}
        for q in range(4):
            dn_before = P.cnt("pe_dn")
            for il in range(16):
                par = il % 2
                if q == 0 and il < 2:
                    P.wait("g", "pe_ex", PEEX)
                else:
                    P.wait("g", "pe_up", up_used[par])
                P.emit("g", lambda e, q=q, il=il, par=par: e.dma_start(
                    self.wfb[par][:],
                    self.upw_d[:, (16 * q + il) * 128:(16 * q + il + 1) * 128]
                    .rearrange("(c p) j -> p c j", p=128)),
                    inc=f"dwu{par}", n=16)
                P.wait("t", f"dwu{par}", P.cnt(f"dwu{par}"))
                self.bank_wait(4, 5)
                for hc in range(NHC):
                    for half in range(2):
                        P.emit("t", lambda e, hc=hc, half=half, par=par:
                               e.matmul(self.pb[4 + half][:],
                                        self.wfb[par][:, hc, :],
                                        self.l2t[:, hc * 1024 + half * 512:
                                                 hc * 1024 + (half + 1) * 512],
                                        start=(hc == 0), stop=(hc == NHC - 1)),
                               inc="pe_up" if (hc == NHC - 1 and half == 1)
                               else None)
                up_used[par] = P.cnt("pe_up")
                P.wait("a", "pe_up", P.cnt("pe_up"))
                for half in range(2):
                    rpar = half
                    P.wait("a", "v_hsq", re_used[rpar])
                    P.emit("a", lambda e, half=half, rpar=rpar: e.activation(
                        self.relust[rpar][:], self.pb[4 + half][:], AF.Relu),
                        inc="a_re")
                    P.wait("v", "a_re", P.cnt("a_re"))
                    if il == 0 and half == 0:
                        if q == 0:
                            P.wait("v", "pe_ex", PEEX)
                        else:
                            P.wait("v", "pe_dn", dn_before)
                    P.emit("v", lambda e, il=il, half=half, rpar=rpar:
                           e.tensor_mul(
                               self.hTq[:, il * 1024 + half * 512:
                                        il * 1024 + (half + 1) * 512],
                               self.relust[rpar][:], self.relust[rpar][:]),
                           inc="v_hsq")
                    re_used[rpar] = P.cnt("v_hsq")
                self.bank_used(4, "a_re")
                self.bank_used(5, "a_re")
            VHSQ = P.drain_inc("v", "v_hsqf")

            dw_used = {0: 0, 1: 0}
            for ocs in range(8):
                dwv = {}
                for c8 in range(2):
                    P.wait("g", "pe_dn", dw_used[c8])
                    P.emit("g", lambda e, q=q, ocs=ocs, c8=c8: e.dma_start(
                        self.dwb[c8][:],
                        self.downw_d[(16 * q + 8 * c8) * 128:
                                     (16 * q + 8 * c8 + 8) * 128,
                                     ocs * 512:(ocs + 1) * 512]
                        .rearrange("(c p) j -> p c j", p=128)),
                        inc=f"ddw{c8}", n=16)
                    dwv[c8] = P.cnt(f"ddw{c8}")
                if q == 3:
                    if ocs == 0:
                        P.wait("g", "pe_up", P.cnt("pe_up"))
                    else:
                        P.wait("g", "pe_dn", dw_used[0])
                    P.emit("g", lambda e, ocs=ocs: e.dma_start(
                        self.fc2b[:],
                        self.fc2w_d[:, ocs * 512:(ocs + 1) * 512]
                        .rearrange("(c p) j -> p c j", p=128)),
                        inc="dfc2", n=16)
                if ocs == 0:
                    P.wait("t", "v_hsqf", VHSQ)
                self.bank_wait(*range(8))
                for il in range(16):
                    c8 = il // 8
                    if il % 8 == 0:
                        P.wait("t", f"ddw{c8}", dwv[c8])
                    for tb in range(8):
                        inc = ("pe_dn" if (q != 3 and il == 15 and tb == 7)
                               else None)
                        P.emit("t", lambda e, il=il, tb=tb, c8=c8, q=q:
                               e.matmul(
                                   self.pb[tb][:],
                                   self.hTq[:, il * 1024 + tb * 128:
                                            il * 1024 + (tb + 1) * 128],
                                   self.dwb[c8][:, il % 8, :],
                                   start=(il == 0),
                                   stop=(q != 3 and il == 15)), inc=inc)
                if q == 3:
                    P.wait("t", "dfc2", P.cnt("dfc2"))
                    if ocs == 0:
                        P.wait("t", "v_moef", VMOE)
                    for lc in range(NLC):
                        for tb in range(8):
                            P.emit("t", lambda e, lc=lc, tb=tb: e.matmul(
                                self.pb[tb][:],
                                self.moeT[:, lc * 1024 + tb * 128:
                                          lc * 1024 + (tb + 1) * 128],
                                self.fc2b[:, lc, :],
                                start=False, stop=(lc == NLC - 1)),
                                inc="pe_dn" if (lc == NLC - 1 and tb == 7)
                                else None)
                pednv = P.cnt("pe_dn")
                dw_used[0] = pednv
                dw_used[1] = pednv
                for tb in range(8):
                    par = tb % 2
                    P.wait("a", "pe_dn", pednv)
                    P.wait("a", f"dpre{par}", pre_used[par])
                    P.emit("a", lambda e, tb=tb, par=par: e.activation(
                        self.prest[par][:], self.pb[tb][:], AF.Copy),
                        inc="a_pre")
                    self.bank_used(tb, "a_pre")
                    P.wait("g", "a_pre", P.cnt("a_pre"))
                    if q > 0 and ocs == 0 and tb == 0:
                        P.wait("g", "dpre0", pre_base[0] + 16 * 32 * q)
                        P.wait("g", "dpre1", pre_base[1] + 16 * 32 * q)
                    op = {} if q == 0 else {"accum_op": ALU.add}
                    P.emit("g", lambda e, tb=tb, ocs=ocs, mb=mb, par=par,
                           op=op: e.dma_start(
                               self.pre_scr[(mb * 8 + tb) * 128:
                                            (mb * 8 + tb + 1) * 128,
                                            ocs * 512:(ocs + 1) * 512],
                               self.prest[par][:], **op), inc=f"dpre{par}",
                           n=16)
                    pre_used[par] = P.cnt(f"dpre{par}")
        self.DPRE = (P.cnt("dpre0"), P.cnt("dpre1"))

    def emit_top2(self, tb):
        P = self
        l = self.pb[7]
        s2 = self.st2
        P.emit("v", lambda e: e.tensor_copy(s2[:, 0:1], l[:, 0:1]))
        for i in range(1, 7):
            P.chain("v")
            P.emit("v", lambda e, i=i: e.tensor_max(
                s2[:, i:i + 1], s2[:, i - 1:i], l[:, i:i + 1]))
        P.emit("v", lambda e: e.tensor_copy(s2[:, 13:14], l[:, 7:8]))
        for j in range(6, 0, -1):
            P.chain("v")
            P.emit("v", lambda e, j=j: e.tensor_max(
                s2[:, 6 + j:7 + j], s2[:, 7 + j:8 + j], l[:, j:j + 1]))
        P.chain("v")
        P.emit("v", lambda e: e.tensor_add(s2[:, 16:17], l[:, 0:1], s2[:, 7:8]))
        P.emit("v", lambda e: e.tensor_add(s2[:, 23:24], l[:, 7:8], s2[:, 6:7]))
        for i in range(1, 7):
            P.emit("v", lambda e, i=i: e.tensor_max(
                s2[:, 24 + i:25 + i], s2[:, i - 1:i], s2[:, 7 + i:8 + i]))
        P.chain("v")
        for i in range(1, 7):
            P.emit("v", lambda e, i=i: e.tensor_add(
                s2[:, 16 + i:17 + i], l[:, i:i + 1], s2[:, 24 + i:25 + i]))
        P.chain("v")
        P.emit("v", lambda e: e.tensor_max(s2[:, 32:33], s2[:, 16:17], s2[:, 17:18]))
        for i in range(2, 8):
            P.chain("v")
            P.emit("v", lambda e, i=i: e.tensor_max(
                s2[:, 32:33], s2[:, 32:33], s2[:, 16 + i:17 + i]))
        P.chain("v")
        P.emit("v", lambda e, tb=tb: e.tensor_copy(
            self.rw_nat[:, tb:tb + 1], s2[:, 32:33]), inc="v_rw")

    def ph_D5(self, mb):
        P = self
        P.emit("s", lambda e: e.dma_start(self.gbc[:], self.gamma_d),
               inc="dgb", n=16)
        P.emit("s", lambda e: e.dma_start(self.bbc[:], self.beta_d),
               inc="dgb", n=16)
        dgb = P.cnt("dgb")
        used = {0: 0, 1: 0}
        dy_used = {0: 0, 1: 0}
        for tb in range(8):
            par = tb % 2
            P.wait("s", "dpre0", self.DPRE[0])
            P.wait("s", "dpre1", self.DPRE[1])
            P.wait("s", "dx2s0", P.cnt("dx2s0"))
            P.wait("s", "dx2s1", P.cnt("dx2s1"))
            if used[par]:
                P.wait("s", "v_y", used[par])
            P.emit("s", lambda e, tb=tb, mb=mb, par=par: e.dma_start(
                self.pret[par][:],
                self.pre_scr[(mb * 8 + tb) * 128:(mb * 8 + tb + 1) * 128, :]),
                inc=f"dpret{par}", n=16)
            P.emit("s", lambda e, tb=tb, mb=mb, par=par: e.dma_start(
                self.x2f[par][:],
                self.x2_scr[(mb * 8 + tb) * 128:(mb * 8 + tb + 1) * 128, :]),
                inc=f"dpret{par}", n=16)
            dv = P.cnt(f"dpret{par}")
            lnv = self.emit_ln_d5(par, ("dpret" + str(par), dv), dgb, tb,
                                  dy_used[par])
            used[par] = lnv
            P.wait("s", "v_y", lnv)
            P.emit("s", lambda e, tb=tb, mb=mb, par=par: e.dma_start(
                self.y_d[(mb * 8 + tb) * 128:(mb * 8 + tb + 1) * 128, :],
                self.yst[par][:]), inc=f"dy{par}", n=16)
            dy_used[par] = P.cnt(f"dy{par}")

    def emit_ln_d5(self, par, src_ready, dgb, tb, dyfree=0):
        P = self
        nm, v = src_ready
        src = self.pret[par]
        stt = self.stt
        P.wait("a", nm, v)
        P.emit("a", lambda e: e.activation(self.tst[:], src[:], AF.Square),
               inc="a_sq")
        P.wait("v", nm, v)
        P.emit("v", lambda e: e.reduce_sum(stt[:, 0:1], src[:], axis=AX.X))
        P.wait("v", "a_sq", P.cnt("a_sq"))
        P.emit("v", lambda e: e.reduce_sum(stt[:, 1:2], self.tst[:], axis=AX.X))
        P.chain("v")
        P.emit("v", lambda e: e.tensor_scalar(
            stt[:, 2:3], stt[:, 0:1], 1.0 / HID, None, ALU.mult))
        P.emit("v", lambda e: e.tensor_scalar(
            stt[:, 3:4], stt[:, 1:2], 1.0 / HID, None, ALU.mult))
        P.chain("v")
        P.emit("v", lambda e: e.tensor_tensor(
            stt[:, 4:5], stt[:, 2:3], stt[:, 2:3], ALU.mult))
        P.chain("v")
        P.emit("v", lambda e: e.tensor_sub(
            stt[:, 5:6], stt[:, 3:4], stt[:, 4:5]), inc="v_var")
        P.wait("a", "v_var", P.cnt("v_var"))
        P.emit("a", lambda e: e.activation(
            stt[:, 6:7], stt[:, 5:6], AF.Ln, bias=self.epsv[:]), inc="a_ln")
        P.wait("a", "a_ln", P.cnt("a_ln"))
        P.emit("a", lambda e: e.activation(
            stt[:, 7:8], stt[:, 6:7], AF.Exp, scale=-0.5), inc="a_rstd")
        P.wait("v", "a_rstd", P.cnt("a_rstd"))
        P.emit("v", lambda e: e.tensor_scalar(
            self.tst[:], src[:], stt[:, 2:3], stt[:, 7:8],
            ALU.subtract, ALU.mult))
        if tb == 0:
            P.wait("v", "dgb", dgb)
        P.chain("v")
        P.emit("v", lambda e: e.tensor_mul(self.tst[:], self.tst[:], self.gbc[:]))
        P.chain("v")
        P.emit("v", lambda e: e.tensor_add(self.tst[:], self.tst[:], self.bbc[:]))
        if dyfree:
            P.wait("v", f"dy{par}", dyfree)
        P.chain("v")
        P.emit("v", lambda e, par=par: e.tensor_add(
            self.yst[par][:], self.tst[:], self.x2f[par][:]), inc="v_y")
        return P.cnt("v_y")

    def build(self):
        self.declare()
        self.ph_init()
        self.ph_A()
        if PHASES_UPTO >= 1:
            self.barrier("A_B0")
            self.ph_B0()
        if PHASES_UPTO >= 2:
            self.barrier("B0_B1")
            self.ph_B1()
        if PHASES_UPTO >= 3:
            for mb in range(2):
                self.barrier(f"pre_C{mb}")
                self.ph_C(mb)
                if PHASES_UPTO >= 4:
                    self.barrier(f"C_D{mb}")
                    self.ph_D(mb)
                    self.barrier(f"D_D5{mb}")
                    self.ph_D5(mb)
        for nm in ("dy0", "dy1", "ddbg", "dlnst0", "dlnst1", "dqts", "dats",
                   "dln2", "dpre0", "dpre1", "dx2s0", "dx2s1"):
            v = self.cnt(nm)
            if v:
                self.wait("g", nm, v)
        blk = self.ctx.enter_context(self.nc.Block())
        secs = {"t": blk.tensor, "v": blk.vector, "a": blk.scalar,
                "s": blk.sync, "g": blk.gpsimd}
        for key in "tvasg":
            ops = self.ops[key]

            def body(e, ops=ops):
                for fn in ops:
                    fn(e)
            secs[key](body)
        return self.nc


_CACHE = {}


def _get_program(taps=()):
    key = (tuple(sorted(taps)), PHASES_UPTO)
    if key not in _CACHE:
        mk = MK(debug_taps=taps)
        nc = mk.build()
        _CACHE[key] = (nc, mk)
    return _CACHE[key]


def kernel(L0_attn_hidden_states, L0_attn_Wqkv, L0_attn_Wo, L0_ffn_gate_w,
           L0_ffn_up_w, L0_ffn_down_w, L0_ffn_fc1_w, L0_ffn_fc2_w,
           L0_ffn_expert_w, L0_ffn_ln_gamma, L0_ffn_ln_beta):
    x = np.ascontiguousarray(np.asarray(L0_attn_hidden_states, dtype=np.float32))
    w = {
        "wqkv": np.ascontiguousarray(np.asarray(L0_attn_Wqkv, np.float32)),
        "wo": np.ascontiguousarray(np.asarray(L0_attn_Wo, np.float32)),
        "gatew": np.ascontiguousarray(np.asarray(L0_ffn_gate_w, np.float32)),
        "upw": np.ascontiguousarray(np.asarray(L0_ffn_up_w, np.float32)),
        "downw": np.ascontiguousarray(np.asarray(L0_ffn_down_w, np.float32)),
        "fc1w": np.ascontiguousarray(np.asarray(L0_ffn_fc1_w, np.float32)),
        "fc2w": np.ascontiguousarray(np.asarray(L0_ffn_fc2_w, np.float32)),
        "expw": np.ascontiguousarray(np.asarray(L0_ffn_expert_w, np.float32)),
    }
    gamma = np.ascontiguousarray(np.broadcast_to(
        np.asarray(L0_ffn_ln_gamma, np.float32), (128, HID)))
    beta = np.ascontiguousarray(np.broadcast_to(
        np.asarray(L0_ffn_ln_beta, np.float32), (128, HID)))
    gtabs = [_g_table(0), _g_table(1)]
    mtabs = [_mask_table(0), _mask_table(1)]

    nc, _mk = _get_program(DEBUG_TAPS)
    in_maps = []
    for c in range(8):
        b, p = c // 2, c % 2
        xb = x[b]
        x_own = np.ascontiguousarray(np.concatenate(
            [xb[512 * gs:512 * (gs + 1)] for gs in STRIPS[p]], axis=0))
        im = {"x_all": np.ascontiguousarray(xb), "x_own": x_own,
              "gamma_bc": gamma, "beta_bc": beta,
              "gtab": gtabs[p], "masktab": mtabs[p]}
        im.update(w)
        in_maps.append(im)

    res = run_bass_kernel_spmd(nc, in_maps, core_ids=list(range(8)))
    kernel._last_results = res.results

    y = np.empty((B, S, HID), np.float32)
    for c in range(8):
        b, p = c // 2, c % 2
        yo = res.results[c]["y"]
        for ss, gs in enumerate(STRIPS[p]):
            y[b, 512 * gs:512 * (gs + 1)] = yo[512 * ss:512 * (ss + 1)]
    return y
